# version 02 (final)
# baseline (speedup 1.0000x reference)
import os
import numpy as np

import concourse.bass as bass
import concourse.mybir as mybir
from concourse.bass_utils import run_bass_kernel_spmd

F32 = mybir.dt.float32
BF16 = mybir.dt.bfloat16
AF = mybir.ActivationFunctionType
ALU = mybir.AluOpType

NCORES = 8
B = 16
S = 2048
D = 1024
BPC = B // NCORES
T = BPC * S
DT = D // 128
CAP = 10240

PARTIAL = int(os.environ.get("ATHENA_PARTIAL", "4"))
SUB = int(os.environ.get("ATHENA_SUB", "99"))
NHP = int(os.environ.get("ATHENA_NHP", "8"))
FFNPART = os.environ.get("ATHENA_FFNPART", "all")
DBG = os.environ.get("ATHENA_DBG", "0") == "1"
NB = int(os.environ.get("ATHENA_NB", "2"))


class Sem:
    def __init__(self, ctx, nc, name):
        self.h = ctx.enter_context(nc.semaphore(name))
        self.n = 0

    def inc(self, instr, k=1):
        instr.then_inc(self.h, k)
        self.n += k
        return instr


GQA_PERM = [0, 4, 1, 5, 2, 6, 3, 7, 8, 12, 9, 13, 10, 14, 11, 15]


def _prep_weights(i):
    w = {}
    w["l0_wqT"] = np.ascontiguousarray(np.asarray(i["L0_attn_Wq"]).T)
    w["l0_wkT"] = np.ascontiguousarray(np.asarray(i["L0_attn_Wk"]).T)
    w["l0_wvT"] = np.ascontiguousarray(np.asarray(i["L0_attn_Wv"]).T)
    for nm, key in (("l0_bq", "L0_attn_bq"), ("l0_bk", "L0_attn_bk"),
                    ("l0_bv", "L0_attn_bv")):
        w[nm] = np.ascontiguousarray(np.asarray(i[key]).reshape(DT, 128).T)
    w["l0_router"] = np.ascontiguousarray(i["L0_ffn_router_w"])
    w["l0_wg"] = np.ascontiguousarray(i["L0_ffn_w_gate"])
    w["l0_wu"] = np.ascontiguousarray(i["L0_ffn_w_up"])
    w["l0_wd"] = np.ascontiguousarray(i["L0_ffn_w_down"])
    w["l0_wsg"] = np.ascontiguousarray(i["L0_ffn_ws_gate"])
    w["l0_wsu"] = np.ascontiguousarray(i["L0_ffn_ws_up"])
    w["l0_wsd"] = np.ascontiguousarray(i["L0_ffn_ws_down"])
    w["l1_inT"] = np.ascontiguousarray(np.asarray(i["L1_attn_in_proj_w"]).T)
    w["l1_conv"] = np.ascontiguousarray(
        np.asarray(i["L1_attn_conv_w"])[:, 0, :].reshape(16, 128, 4)
        .transpose(1, 0, 2).reshape(128, 64))
    w["l1_convb"] = np.ascontiguousarray(np.asarray(i["L1_attn_conv_b"]).reshape(16, 128).T)
    w["l1_xpT"] = np.ascontiguousarray(np.asarray(i["L1_attn_x_proj_w"]).T)
    w["l1_dtT"] = np.ascontiguousarray(np.asarray(i["L1_attn_dt_proj_w"]).T)
    w["l1_dtb"] = np.ascontiguousarray(np.asarray(i["L1_attn_dt_proj_b"]).reshape(16, 128).T)
    w["l1_alog"] = np.ascontiguousarray(
        np.asarray(i["L1_attn_A_log"]).reshape(16, 128, 16)
        .transpose(1, 0, 2).reshape(128, 256))
    w["l1_dvec"] = np.ascontiguousarray(np.asarray(i["L1_attn_D"]).reshape(16, 128).T)
    w["l1_outT"] = np.ascontiguousarray(np.asarray(i["L1_attn_out_proj_w"]).T)
    _gperm = [0, 4, 1, 5, 2, 6, 3, 7, 8, 12, 9, 13, 10, 14, 11, 15]
    _qcols = np.concatenate([np.arange(h * 64, (h + 1) * 64) for h in _gperm])
    w["l1_qw"] = np.ascontiguousarray(np.asarray(i["L1_ffn_qw"])[:, _qcols])
    w["l1_kw"] = np.ascontiguousarray(i["L1_ffn_kw"])
    w["l1_vw"] = np.ascontiguousarray(i["L1_ffn_vw"])
    w["l1_ow"] = np.ascontiguousarray(i["L1_ffn_ow"])
    w["l1_qb"] = np.ascontiguousarray(
        np.asarray(i["L1_ffn_qb"])[_qcols].reshape(DT, 128).T)
    w["l1_kb"] = np.ascontiguousarray(np.asarray(i["L1_ffn_kb"]).reshape(2, 128).T)
    w["l1_vb"] = np.ascontiguousarray(np.asarray(i["L1_ffn_vb"]).reshape(2, 128).T)
    w["l1_ob"] = np.ascontiguousarray(np.asarray(i["L1_ffn_ob"]).reshape(DT, 128).T)
    w["l1_gate"] = np.ascontiguousarray(i["L1_ffn_gate_w"])
    w["l1_w1"] = np.ascontiguousarray(i["L1_ffn_w1"])
    w["l1_w2"] = np.ascontiguousarray(i["L1_ffn_w2"])
    w["l1_n1"] = np.ascontiguousarray(np.asarray(i["L1_ffn_norm1_w"]).reshape(DT, 128).T)
    w["l1_n2"] = np.ascontiguousarray(np.asarray(i["L1_ffn_norm2_w"]).reshape(DT, 128).T)
    return w


SHAPES = {
    "l0_wqT": [D, D], "l0_wkT": [D, D], "l0_wvT": [D, D],
    "l0_bq": [128, DT], "l0_bk": [128, DT], "l0_bv": [128, DT],
    "l0_router": [D, 8],
    "l0_wg": [8, D, 512], "l0_wu": [8, D, 512], "l0_wd": [8, 512, D],
    "l0_wsg": [D, D], "l0_wsu": [D, D], "l0_wsd": [D, D],
    "l1_inT": [D, 4096], "l1_conv": [128, 64], "l1_convb": [128, 16],
    "l1_xpT": [2048, 96], "l1_dtT": [64, 2048], "l1_dtb": [128, 16],
    "l1_alog": [128, 256], "l1_dvec": [128, 16], "l1_outT": [2048, D],
    "l1_qw": [D, D], "l1_kw": [D, 256], "l1_vw": [D, 256], "l1_ow": [D, D],
    "l1_qb": [128, DT], "l1_kb": [128, 2], "l1_vb": [128, 2], "l1_ob": [128, DT],
    "l1_gate": [D, 8], "l1_w1": [8, D, 2048], "l1_w2": [8, 2048, D],
    "l1_n1": [128, DT], "l1_n2": [128, DT],
}

SMALLW = ["l0_bq", "l0_bk", "l0_bv", "l1_conv", "l1_convb", "l1_dtb",
          "l1_alog", "l1_dvec", "l1_qb", "l1_kb", "l1_vb", "l1_ob",
          "l1_n1", "l1_n2"]


def build_program():
    from contextlib import ExitStack
    nc = bass.Bass("TRN2", target_bir_lowering=False)

    x_in = nc.dram_tensor("x_in", [D, T], F32, kind="ExternalInput").ap()
    out_t = nc.dram_tensor("out", [D, T], F32, kind="ExternalOutput").ap()
    dr = {nm: nc.dram_tensor(nm, shp, F32, kind="ExternalInput").ap()
          for nm, shp in SHAPES.items()}
    pmask = nc.dram_tensor("pmask", [8, 1], F32, kind="ExternalInput").ap()

    dbg1 = nc.dram_tensor("dbg1", [128, 32], F32, kind="ExternalOutput").ap() if DBG else None
    dbg2 = nc.dram_tensor("dbg2", [1, 4096], F32, kind="ExternalOutput").ap() if DBG else None
    dbg3 = nc.dram_tensor("dbg3", [128, 512], F32, kind="ExternalOutput").ap() if DBG else None
    dbg4 = nc.dram_tensor("dbg4", [128, 64], F32, kind="ExternalOutput").ap() if DBG else None
    dbg5 = nc.dram_tensor("dbg5", [128, 32], F32, kind="ExternalOutput").ap() if DBG else None
    dbg6 = nc.dram_tensor("dbg6", [128, 4096], F32, kind="ExternalOutput").ap() if DBG else None
    dbg7 = nc.dram_tensor("dbg7", [8, 4096], F32, kind="ExternalOutput").ap() if DBG else None
    dbg8 = nc.dram_tensor("dbg8", [8, 4096], F32, kind="ExternalOutput").ap() if DBG else None
    dbg9 = nc.dram_tensor("dbg9", [8, 16], F32, kind="ExternalOutput").ap() if DBG else None
    dbg10 = nc.dram_tensor("dbg10", [8, 1], F32, kind="ExternalOutput").ap() if DBG else None
    dbg11 = nc.dram_tensor("dbg11", [128, 4096], F32, kind="ExternalOutput").ap() if DBG else None
    dbg12 = nc.dram_tensor("dbg12", [128, 4096], F32, kind="ExternalOutput").ap() if DBG else None
    dbg13 = nc.dram_tensor("dbg13", [128, 2048], F32, kind="ExternalOutput").ap() if DBG else None
    dbg14 = nc.dram_tensor("dbg14", [128, 2048], F32, kind="ExternalOutput").ap() if DBG else None
    dbg15 = nc.dram_tensor("dbg15", [128, 2048], F32, kind="ExternalOutput").ap() if DBG else None
    dbg16 = nc.dram_tensor("dbg16", [64, 2048], F32, kind="ExternalOutput").ap() if DBG else None
    dbg17 = nc.dram_tensor("dbg17", [1, 2048], F32, kind="ExternalOutput").ap() if DBG else None
    xspill = nc.dram_tensor("xspill", [D, T], F32).ap()
    mout = nc.dram_tensor("mout", [D, T], F32).ap()
    x3comb = nc.dram_tensor("x3comb", [D, T], F32).ap()
    vw_dram = nc.dram_tensor("vw_dram", [8, T], BF16).ap()
    ag_in_t = nc.dram_tensor("ag_in", [16, 128], F32)
    ag_out_t = nc.dram_tensor("ag_out", [128, 128], F32)

    ctx = ExitStack()
    with ctx:
        sems = {nm: Sem(ctx, nc, nm) for nm in
                ["qS", "qG", "pe", "act", "dve", "gps", "cc"]}
        qS, qG = sems["qS"], sems["qG"]
        sPE, sACT, sDVE, sGPS = sems["pe"], sems["act"], sems["dve"], sems["gps"]
        sCC = sems["cc"]

        block = ctx.enter_context(nc.Block())

        _uid = [0]

        def U(base):
            _uid[0] += 1
            return "%s_%d" % (base, _uid[0])

        onesb = ctx.enter_context(nc.sbuf_tensor(U("onesb"), [128, 128], BF16))
        onesf = ctx.enter_context(nc.sbuf_tensor(U("onesf"), [1, 128], F32))
        ident = ctx.enter_context(nc.sbuf_tensor(U("ident"), [128, 128], BF16))
        identF = ctx.enter_context(nc.sbuf_tensor(U("identF"), [128, 128], F32))
        statm = ctx.enter_context(nc.sbuf_tensor(U("statm"), [1, 1024], F32))
        statr = ctx.enter_context(nc.sbuf_tensor(U("statr"), [1, 1024], F32))
        stat1 = ctx.enter_context(nc.sbuf_tensor(U("stat1"), [1, 512], F32))
        smallw = {nm: ctx.enter_context(nc.sbuf_tensor(U("sw_") + nm, SHAPES[nm], F32))
                  for nm in SMALLW}

        with nc.sbuf_tensor(U("colidx"), [128, 128], F32) as colidx, \
             nc.sbuf_tensor(U("rowidx"), [128, 1], F32) as rowidx, \
             nc.sbuf_tensor(U("identf"), [128, 128], F32) as identf:

            @block.gpsimd
            def _(g):
                g.memset(onesb[:], 1.0)
                g.memset(onesf[:], 1.0)
                g.iota(colidx[:], pattern=[[1, 128]], channel_multiplier=0,
                       allow_small_or_imprecise_dtypes=True)
                sGPS.inc(g.iota(rowidx[:], pattern=[[1, 1]], channel_multiplier=1,
                                allow_small_or_imprecise_dtypes=True))
                for nm in SMALLW:
                    qG.inc(g.dma_start(out=smallw[nm][:], in_=dr[nm]), 16)

            @block.vector
            def _(v):
                v.wait_ge(sGPS.h, sGPS.n)
                v.tensor_scalar(identf[:], colidx[:], rowidx[:, 0:1], None,
                                ALU.is_equal)
                v.tensor_copy(identF[:], identf[:])
                sDVE.inc(v.tensor_copy(ident[:], identf[:]))


        reg1 = ExitStack()
        xr = [reg1.enter_context(nc.sbuf_tensor(U("xr%d" % j), [128, T], F32))
              for j in range(DT)]

        @block.sync
        def _(s):
            for j in range(DT):
                qS.inc(s.dma_start(out=xr[j][:], in_=x_in[j * 128:(j + 1) * 128, :]), 16)
        x_loaded = qS.n


        def psum_guard_pt():
            return (sACT.n, sDVE.n)

        def guard_new_psum(t, gv):
            t.wait_ge(sACT.h, gv[0])
            t.wait_ge(sDVE.h, gv[1])

        def ln_stats(src_tiles, kind, eps, win, extra_waits=(), scratch_bf16=None):
            per = 2 if kind == "ln" else 1
            gv = psum_guard_pt()
            W = win.stop - win.start
            with nc.psum_tensor(U("ps_s1"), [128, 512], F32) as ps1, \
                 nc.psum_tensor(U("ps_s2"), [128, 512], F32) as ps2, \
                 ExitStack() as _lnctx:
                if scratch_bf16 is not None:
                    cb = scratch_bf16[:, 0:512]
                    sqb = scratch_bf16[:, 512:1024]
                else:
                    cb = _lnctx.enter_context(nc.sbuf_tensor(U("lncb"), [128, 512], BF16))[:]
                    sqb = _lnctx.enter_context(nc.sbuf_tensor(U("lnsq"), [128, 512], BF16))[:]
                vrow = _lnctx.enter_context(nc.sbuf_tensor(U("lnv"), [1, 512], F32))[:]
                for fc in range(W // 512):
                    sl = slice(win.start + fc * 512, win.start + (fc + 1) * 512)
                    lsl = slice(fc * 512, (fc + 1) * 512)
                    base_dve = sDVE.n
                    base_pe = sPE.n

                    @block.vector
                    def _(v, fc=fc, sl=sl, base_pe=base_pe):
                        if fc == 0:
                            for (sh, sv) in extra_waits:
                                v.wait_ge(sh.h, sv)
                        for j in range(DT):
                            v.wait_ge(sPE.h, base_pe)
                            if kind == "ln":
                                sDVE.inc(v.tensor_copy(cb, src_tiles[j][:, sl]))
                            sDVE.inc(v.tensor_tensor(
                                sqb, src_tiles[j][:, sl], src_tiles[j][:, sl],
                                ALU.mult))
                            base_pe += per

                    @block.tensor
                    def _(t, fc=fc, base_dve=base_dve):
                        if fc == 0:
                            guard_new_psum(t, gv)
                        for j in range(DT):
                            t.wait_ge(sDVE.h, base_dve + (j + 1) * per)
                            if kind == "ln":
                                t.matmul(ps1[:], onesb[:], cb,
                                         start=(j == 0), stop=(j == DT - 1))
                            ins = t.matmul(ps2[:], onesb[:], sqb,
                                           start=(j == 0), stop=(j == DT - 1))
                            sPE.inc(ins, per)
                    pe_done = sPE.n

                    @block.vector
                    def _(v, lsl=lsl, pe_done=pe_done):
                        v.wait_ge(sPE.h, pe_done)
                        if kind == "ln":
                            v.tensor_scalar(statm[0:1, lsl], ps1[0:1, :], 1.0 / D,
                                            None, ALU.mult)
                            v.tensor_scalar(vrow, ps2[0:1, :], 1.0 / D,
                                            None, ALU.mult)
                            v.tensor_tensor(stat1[0:1, :], statm[0:1, lsl],
                                            statm[0:1, lsl], ALU.mult)
                            v.tensor_sub(vrow, vrow, stat1[0:1, :])
                            sDVE.inc(v.tensor_scalar_add(vrow, vrow, float(eps)))
                        else:
                            v.tensor_scalar(vrow, ps2[0:1, :], 1.0 / D,
                                            None, ALU.mult)
                            sDVE.inc(v.tensor_scalar_add(vrow, vrow, float(eps)))
                    dve_done = sDVE.n

                    @block.scalar
                    def _(sc, dve_done=dve_done):
                        sc.wait_ge(sDVE.h, dve_done)
                        sACT.inc(sc.activation(stat1[0:1, :], vrow, AF.Sqrt))
                    act_done = sACT.n

                    @block.vector
                    def _(v, lsl=lsl, act_done=act_done):
                        v.wait_ge(sACT.h, act_done)
                        sDVE.inc(v.reciprocal(statr[0:1, lsl], stat1[0:1, :]))

        def bcast_rows(t, ps_m, ps_r, lsl, which="mr"):
            n = lsl.stop - lsl.start
            assert n <= 512
            if "m" in which:
                sPE.inc(t.matmul(ps_m[:, 0:n], onesf[:], statm[0:1, lsl],
                                 start=True, stop=True))
            if "r" in which:
                sPE.inc(t.matmul(ps_r[:, 0:n], onesf[:], statr[0:1, lsl],
                                 start=True, stop=True))

        def emit_l0_attn():
            with nc.sbuf_tensor(U("xhb"), [128, DT * 2048], BF16) as xhb, \
                 nc.sbuf_tensor(U("wsl"), [128, 3 * 1024], BF16) as wsl, \
                 nc.sbuf_tensor(U("q2"), [128, 2048], BF16) as q2, \
                 nc.sbuf_tensor(U("k2"), [128, 2048], BF16) as k2, \
                 nc.sbuf_tensor(U("v2"), [128, 2048], BF16) as v2, \
                 nc.sbuf_tensor(U("vte"), [128, 16 * 130], BF16) as vte, \
                 nc.sbuf_tensor(U("attn0"), [128, 2048], BF16) as attn0, \
                 nc.sbuf_tensor(U("attn1"), [128, 2048], BF16) as attn1, \
                 nc.sbuf_tensor(U("lntmp"), [128, 512], F32) as lntmp:
                ctxsb = attn0[0:64, :]
                t1sb = lntmp[0:64, :]

                wsl_ready = {}
                proj_end = {}
                sched = [(b_, hp_) for b_ in range(min(BPC, NB))
                         for hp_ in range(min(8, NHP))]

                def load_wsl(idx):
                    if idx >= len(sched):
                        return
                    hp = sched[idx][1]
                    bi = 0
                    guard = proj_end.get(idx - 1)

                    @block.gpsimd
                    def _(g):
                        if guard is not None:
                            g.wait_ge(sPE.h, guard)
                        for mi, nm in enumerate(("l0_wqT", "l0_wkT", "l0_wvT")):
                            src = dr[nm][:, hp * 128:(hp + 1) * 128] \
                                .rearrange("(c p) o -> p c o", p=128)
                            dst = wsl[:, bi * 3072 + mi * 1024:
                                      bi * 3072 + (mi + 1) * 1024] \
                                .rearrange("p (c o) -> p c o", c=DT)
                            qG.inc(g.dma_start(out=dst, in_=src), 16)
                    wsl_ready[idx] = qG.n

                load_wsl(0)

                for b in range(min(BPC, NB)):
                    if SUB < 1:
                        break
                    bs = slice(b * 2048, (b + 1) * 2048)
                    pe_before = sPE.n
                    for sub in range(2):
                        win = slice(bs.start + sub * 1024, bs.start + (sub + 1) * 1024)
                        ln_stats(xr, "ln", 1e-5, win=win,
                                 extra_waits=[(qS, x_loaded)] if (b == 0 and sub == 0) else (),
                                 scratch_bf16=attn1)

                        with nc.psum_tensor(U("ps_m"), [128, 512], F32) as psm, \
                             nc.psum_tensor(U("ps_r"), [128, 512], F32) as psr:
                            gvx = psum_guard_pt()
                            for fc in range(2 if SUB >= 2 else 0):
                                lsl = slice(fc * 512, (fc + 1) * 512)
                                sl = slice(win.start + fc * 512, win.start + (fc + 1) * 512)
                                dve_prev = sDVE.n

                                @block.tensor
                                def _(t, dve_prev=dve_prev, lsl=lsl, fc=fc):
                                    if fc == 0:
                                        guard_new_psum(t, gvx)
                                    t.wait_ge(sDVE.h, dve_prev)
                                    bcast_rows(t, psm, psr, lsl)
                                pe_done = sPE.n

                                @block.vector
                                def _(v, sl=sl, sub=sub, lsl=lsl, pe_done=pe_done,
                                      pe_before=pe_before):
                                    v.wait_ge(sPE.h, max(pe_done, pe_before))
                                    loc = sub * 1024 + lsl.start
                                    for j in range(DT):
                                        v.tensor_sub(lntmp[:], xr[j][:, sl], psm[:])
                                        sDVE.inc(v.tensor_tensor(
                                            xhb[:, j * 2048 + loc:j * 2048 + loc + 512],
                                            lntmp[:], psr[:], ALU.mult))
                    xh_done = sDVE.n

                    for hp in range(min(8, NHP)):
                        if SUB < 3:
                            break
                        bhp = sched.index((b, hp))
                        bi = 0

                        with nc.psum_tensor(U("ps_p0"), [128, 512], F32) as pp0, \
                             nc.psum_tensor(U("ps_p1"), [128, 512], F32) as pp1:
                            gvp = psum_guard_pt()
                            base_act = sACT.n
                            pps = [pp0, pp1]

                            @block.tensor
                            def _(t, bi=bi, base_act=base_act, bhp=bhp):
                                guard_new_psum(t, gvp)
                                t.wait_ge(sDVE.h, xh_done)
                                t.wait_ge(qG.h, wsl_ready[bhp])
                                k_ = 0
                                for mi in range(3):
                                    for fc in range(4):
                                        ps = pps[k_ % 2]
                                        if k_ >= 2:
                                            t.wait_ge(sACT.h, base_act + k_ - 1)
                                        for c in range(DT):
                                            ins = t.matmul(
                                                ps[:],
                                                wsl[:, bi * 3072 + mi * 1024 + c * 128:
                                                    bi * 3072 + mi * 1024 + (c + 1) * 128],
                                                xhb[:, c * 2048 + fc * 512:
                                                    c * 2048 + (fc + 1) * 512],
                                                start=(c == 0), stop=(c == DT - 1))
                                        sPE.inc(ins)
                                        k_ += 1
                            proj_end[bhp] = sPE.n
                            load_wsl(bhp + 1)
                            pe_base = sPE.n - 12

                            @block.scalar
                            def _(sc, hp=hp, pe_base=pe_base):
                                k_ = 0
                                for (dst, bnm) in ((q2, "l0_bq"), (k2, "l0_bk"),
                                                   (v2, "l0_bv")):
                                    for fc in range(4):
                                        ps = pps[k_ % 2]
                                        sc.wait_ge(sPE.h, pe_base + k_ + 1)
                                        sACT.inc(sc.activation(
                                            dst[:, fc * 512:(fc + 1) * 512], ps[:],
                                            AF.Identity,
                                            bias=smallw[bnm][:, hp:hp + 1]))
                                        k_ += 1
                        proj_done = sACT.n

                        if SUB < 4:
                            continue
                        with nc.psum_tensor(U("ps_tr"), [128, 1024], BF16) as pst:
                            gvt = psum_guard_pt()
                            pe_at = sPE.n
                            dve_b0 = sDVE.n

                            @block.vector
                            def _(v, pe_at=pe_at):
                                v.wait_ge(sACT.h, proj_done)
                                v.wait_ge(sPE.h, pe_at)
                                sDVE.inc(v.memset(vte[:], 1.0))

                            @block.tensor
                            def _(t, dve_b0=dve_b0):
                                guard_new_psum(t, gvt)
                                for scn in range(16):
                                    t.wait_ge(sDVE.h, dve_b0 + 1 + scn)
                                    sPE.inc(t.transpose(
                                        pst[:, 0:128], v2[:, scn * 128:(scn + 1) * 128],
                                        ident[:]))
                            tr_base = sPE.n - 16

                            @block.vector
                            def _(v, tr_base=tr_base):
                                for scn in range(16):
                                    v.wait_ge(sPE.h, tr_base + scn + 1)
                                    v.tensor_copy(
                                        vte[:, scn * 130:scn * 130 + 64],
                                        pst[:, 0:64])
                                    sDVE.inc(v.tensor_copy(
                                        vte[:, scn * 130 + 65:scn * 130 + 129],
                                        pst[:, 64:128]))

                        for h2 in range(2):
                            if SUB < 5:
                                break
                            h = hp * 2 + h2
                            hr = slice(h2 * 64, h2 * 64 + 64)
                            with nc.psum_tensor(U("ps_qk0"), [128, 1024], F32) as pq0, \
                                 nc.psum_tensor(U("ps_qk1"), [128, 1024], F32) as pq1, \
                                 nc.psum_tensor(U("ps_ctx"), [65, 2048], F32) as pctx:
                                qk_ps = [pq0, pq1]
                                attn_sb = [attn0, attn1]
                                gva = psum_guard_pt()
                                base_pe = sPE.n
                                base_act = sACT.n
                                dve_at = sDVE.n

                                def pe_cnt(scn, half, base_pe=base_pe):
                                    c = 0
                                    for s_ in range(scn):
                                        c += 4 if s_ == 0 else 8
                                    c += (half + 1) * 2
                                    return base_pe + c

                                @block.tensor
                                def _(t, h2=h2, hr=hr, base_act=base_act,
                                      dve_at=dve_at):
                                    guard_new_psum(t, gva)
                                    t.wait_ge(sDVE.h, dve_at)
                                    for scn in range(16):
                                        for half in range(2):
                                            if scn >= 2:
                                                t.wait_ge(
                                                    sACT.h,
                                                    base_act + (scn - 2) * 2 + half + 1)
                                            for q4 in range(2):
                                                ins = t.matmul(
                                                    qk_ps[half][:, q4 * 512:(q4 + 1) * 512],
                                                    k2[hr, scn * 128:(scn + 1) * 128],
                                                    q2[hr, half * 1024 + q4 * 512:
                                                       half * 1024 + (q4 + 1) * 512],
                                                    start=True, stop=True)
                                            sPE.inc(ins, 2)
                                        if scn >= 1:
                                            pscn = scn - 1
                                            t.wait_ge(sACT.h, base_act + pscn * 2 + 2)
                                            for q4 in range(4):
                                                ins = t.matmul(
                                                    pctx[:, q4 * 512:(q4 + 1) * 512],
                                                    vte[:, pscn * 130 + h2 * 65:
                                                        pscn * 130 + h2 * 65 + 65],
                                                    attn_sb[pscn % 2][:, q4 * 512:(q4 + 1) * 512],
                                                    start=(pscn == 0), stop=False)
                                            sPE.inc(ins, 4)
                                    t.wait_ge(sACT.h, base_act + 15 * 2 + 2)
                                    for q4 in range(4):
                                        ins = t.matmul(
                                            pctx[:, q4 * 512:(q4 + 1) * 512],
                                            vte[:, 15 * 130 + h2 * 65:
                                                15 * 130 + h2 * 65 + 65],
                                            attn_sb[1][:, q4 * 512:(q4 + 1) * 512],
                                            start=False, stop=True)
                                    sPE.inc(ins, 4)

                                @block.scalar
                                def _(sc):
                                    for scn in range(16):
                                        for half in range(2):
                                            sc.wait_ge(sPE.h, pe_cnt(scn, half))
                                            sACT.inc(sc.activation(
                                                attn_sb[scn % 2][:, half * 1024:
                                                                 (half + 1) * 1024],
                                                qk_ps[half][:], AF.Exp, scale=0.125))

                                av_done = sPE.n
                                dve_b = sDVE.n

                                @block.vector
                                def _(v):
                                    v.wait_ge(sPE.h, av_done)
                                    v.reciprocal(statm[0:1, :], pctx[64:65, 0:1024])
                                    sDVE.inc(v.reciprocal(statr[0:1, :], pctx[64:65, 1024:2048]))

                                @block.tensor
                                def _(t, dve_b=dve_b):
                                    t.wait_ge(sDVE.h, dve_b + 1)
                                    for q4 in range(4):
                                        ps = pq0 if q4 < 2 else pq1
                                        row = statm if q4 < 2 else statr
                                        sPE.inc(t.matmul(
                                            ps[0:64, (q4 % 2) * 512:(q4 % 2) * 512 + 512],
                                            onesf[:, 0:64],
                                            row[0:1, (q4 % 2) * 512:(q4 % 2) * 512 + 512],
                                            start=True, stop=True))
                                rb_done = sPE.n

                                @block.scalar
                                def _(sc):
                                    sc.wait_ge(sPE.h, rb_done)
                                    sACT.inc(sc.copy(ctxsb, pctx[0:64, :]))
                                ctx_done = sACT.n

                                j = h // 2
                                hi = (h % 2) == 1
                                for q4 in range(4):
                                    ps = pq0 if q4 < 2 else pq1
                                    sl = slice(q4 * 512, (q4 + 1) * 512)
                                    xsl = slice(bs.start + q4 * 512,
                                                bs.start + (q4 + 1) * 512)
                                    psl = slice((q4 % 2) * 512, (q4 % 2) * 512 + 512)

                                    @block.vector
                                    def _(v, sl=sl, psl=psl, ps=ps, q4=q4):
                                        if q4 == 0:
                                            v.wait_ge(sACT.h, ctx_done)
                                        sDVE.inc(v.tensor_tensor(
                                            lntmp[0:64, :], attn0[0:64, sl],
                                            ps[0:64, psl], ALU.mult))
                                    mul_done = sDVE.n

                                    if hi:
                                        @block.sync
                                        def _(s_, mul_done=mul_done):
                                            s_.wait_ge(sDVE.h, mul_done)
                                            qS.inc(s_.dma_start(
                                                out=lntmp[64:128, :],
                                                in_=lntmp[0:64, :]), 16)
                                        dma_done = qS.n

                                        @block.vector
                                        def _(v, xsl=xsl, dma_done=dma_done):
                                            v.wait_ge(qS.h, dma_done)
                                            sDVE.inc(v.tensor_add(
                                                xr[j][64:128, xsl],
                                                xr[j][64:128, xsl],
                                                lntmp[64:128, :]))
                                    else:
                                        @block.vector
                                        def _(v, xsl=xsl):
                                            sDVE.inc(v.tensor_add(
                                                xr[j][0:64, xsl],
                                                xr[j][0:64, xsl],
                                                lntmp[0:64, :]))

        def emit_l0_ffn():
            FCW = 512
            _e1 = ExitStack()
            with _e1:
                xhc = _e1.enter_context(nc.sbuf_tensor(U("xhc"), [128, DT * FCW], BF16))[:]
                hsc = _e1.enter_context(nc.sbuf_tensor(U("hsc"), [128, 8 * FCW], BF16))[:]
                rw_sb = _e1.enter_context(nc.sbuf_tensor(U("rw_sb"), [128, 64], BF16))[:]
                w0em = _e1.enter_context(nc.sbuf_tensor(U("w0em"), [1, 8 * FCW], BF16))[:]
                w0tmp = _e1.enter_context(nc.sbuf_tensor(U("w0tmp"), [8, FCW], BF16))[:]
                ptm = _e1.enter_context(nc.sbuf_tensor(U("ptm"), [128, 32], F32))[:]
                prec = _e1.enter_context(nc.sbuf_tensor(U("prec"), [128, 4], F32))[:]
                pw8 = _e1.enter_context(nc.sbuf_tensor(U("pw8"), [128, 8], F32))[:]
                mx8 = _e1.enter_context(nc.sbuf_tensor(U("mx8"), [128, 8], F32))[:]
                msk8 = _e1.enter_context(nc.sbuf_tensor(U("msk8"), [128, 8], F32))[:]
                gsil = _e1.enter_context(nc.sbuf_tensor(U("gsil"), [128, FCW], BF16))[:]
                lntm2 = _e1.enter_context(nc.sbuf_tensor(U("lntm2"), [128, 512], F32))[:]
                dbgstash = _e1.enter_context(nc.sbuf_tensor(U("dbgst"), [128, 32], F32))[:] if DBG else None
                

                ph_guard = (sPE.n, sDVE.n, sACT.n)

                @block.gpsimd
                def _(g):
                    g.wait_ge(sPE.h, ph_guard[0])
                    g.wait_ge(sDVE.h, ph_guard[1])
                    g.wait_ge(sACT.h, ph_guard[2])
                    src = dr["l0_router"].rearrange("(c p) o -> p c o", p=128)
                    qG.inc(g.dma_start(
                        out=rw_sb.rearrange("p (c o) -> p c o", c=DT), in_=src), 16)
                rw_ready = qG.n

                for fc in range(T // FCW):
                    win = slice(fc * FCW, (fc + 1) * FCW)
                    ln_stats(xr, "ln", 1e-5, win=win, scratch_bf16=xhc[:, 0:1024])

                    _e2 = ExitStack()
                    with _e2:
                        psm = _e2.enter_context(nc.psum_tensor(U("psm"), [128, 512], F32))[:]
                        psr = _e2.enter_context(nc.psum_tensor(U("psr"), [128, 512], F32))[:]
                        gvx = psum_guard_pt()
                        dve_prev = sDVE.n

                        @block.tensor
                        def _(t, dve_prev=dve_prev):
                            guard_new_psum(t, gvx)
                            t.wait_ge(sDVE.h, dve_prev)
                            bcast_rows(t, psm, psr, slice(0, 512))
                        pe_done = sPE.n

                        @block.vector
                        def _(v, win=win, pe_done=pe_done):
                            v.wait_ge(sPE.h, pe_done)
                            for j in range(DT):
                                v.tensor_sub(lntm2[:], xr[j][:, win], psm)
                                sDVE.inc(v.tensor_tensor(
                                    xhc[:, j * FCW:(j + 1) * FCW],
                                    lntm2[:], psr, ALU.mult))
                        xh_done = sDVE.n

                        _e3 = ExitStack()
                        with _e3:
                            pslg = _e3.enter_context(nc.psum_tensor(U("pslg"), [128, 512], F32))[:]
                            pstr = _e3.enter_context(nc.psum_tensor(U("pstr"), [8, 512], F32))[:]
                            gvr = psum_guard_pt()
                            act_b = sACT.n
                            lg_exp = {tch: act_b + tch + 1 for tch in range(4)}

                            @block.tensor
                            def _(t):
                                guard_new_psum(t, gvr)
                                t.wait_ge(sDVE.h, xh_done)
                                t.wait_ge(qG.h, rw_ready)
                                for tch in range(4):
                                    if tch >= 1:
                                        t.wait_ge(sACT.h, lg_exp[tch - 1])
                                    for c in range(DT):
                                        ins = t.matmul(
                                            pslg[:, 0:8],
                                            xhc[:, c * FCW + tch * 128:
                                                c * FCW + (tch + 1) * 128],
                                            rw_sb[:, c * 8:(c + 1) * 8],
                                            start=(c == 0), stop=(c == DT - 1))
                                    sPE.inc(ins)
                            lg_done = sPE.n
                            pe_lg0 = lg_done - 4

                            @block.scalar
                            def _(sc):
                                for tch in range(4):
                                    sc.wait_ge(sPE.h, pe_lg0 + tch + 1)
                                    sACT.inc(sc.activation(
                                        ptm[:, tch * 8:(tch + 1) * 8],
                                        pslg[:, 0:8], AF.Exp))

                            dve_r0 = sDVE.n

                            if DBG and fc == 0:
                                @block.vector
                                def _(v):
                                    v.wait_ge(sACT.h, act_b + 4)
                                    sDVE.inc(v.tensor_copy(dbgstash[:], ptm[:]))

                            @block.vector
                            def _(v):
                                v.wait_ge(sACT.h, act_b + 4)
                                for tch in range(4):
                                    v.tensor_reduce(
                                        prec[:, tch:tch + 1],
                                        ptm[:, tch * 8:(tch + 1) * 8],
                                        mybir.AxisListType.X, ALU.add)
                                v.drain()
                                v.reciprocal(prec[:], prec[:])
                                v.drain()
                                for tch in range(4):
                                    tsl = slice(tch * 8, (tch + 1) * 8)
                                    v.tensor_scalar(pw8[:], ptm[:, tsl],
                                                    prec[:, tch:tch + 1], None,
                                                    ALU.mult)
                                    v.drain()
                                    v.max(out=mx8[:], in_=pw8[:])
                                    v.drain()
                                    v.tensor_scalar(msk8[:], pw8[:], mx8[:, 5:6],
                                                    None, ALU.is_ge)
                                    v.drain()
                                    sDVE.inc(v.tensor_tensor(ptm[:, tsl], pw8[:],
                                                             msk8[:], ALU.mult))
                                    v.drain()
                            w0tm_done = sDVE.n
                            w0tm_evac = {tch: w0tm_done + tch + 1
                                         for tch in range(4)}

                            @block.tensor
                            def _(t):
                                t.wait_ge(sDVE.h, w0tm_done)
                                for tch in range(4):
                                    if tch > 0:
                                        t.wait_ge(sDVE.h, w0tm_evac[tch - 1])
                                    sPE.inc(t.transpose(
                                        pstr[:, 0:128], ptm[:, tch * 8:(tch + 1) * 8],
                                        identF[:]))
                            tr_done0 = sPE.n - 4

                            @block.vector
                            def _(v):
                                for tch in range(4):
                                    v.wait_ge(sPE.h, tr_done0 + tch + 1)
                                    sDVE.inc(v.tensor_copy(
                                        w0tmp[:, tch * 128:(tch + 1) * 128],
                                        pstr[:, 0:128]))
                            w0tm2 = sDVE.n

                            @block.sync
                            def _(s_):
                                s_.wait_ge(sDVE.h, w0tm2)
                                for e in range(8):
                                    qS.inc(s_.dma_start(
                                        out=w0em[0:1, e * FCW:(e + 1) * FCW],
                                        in_=w0tmp[e:e + 1, :]), 16)
                        w0_done = qS.n

                        if DBG and fc == 0:
                            with nc.sbuf_tensor(U("dbw"), [1, 4096], F32) as dbw, \
                                 nc.sbuf_tensor(U("dbx2"), [128, 64], F32) as dbx2, \
                                 nc.sbuf_tensor(U("dbx"), [128, 512], F32) as dbx:
                                dvq = sDVE.n

                                @block.vector
                                def _(v):
                                    v.wait_ge(qS.h, w0_done)
                                    v.tensor_copy(dbw[:], w0em)
                                    sDVE.inc(v.tensor_copy(dbx[:], xhc[:, 0:512]))

                                @block.vector
                                def _(v):
                                    sDVE.inc(v.tensor_copy(dbx2[:], rw_sb))

                                @block.sync
                                def _(s_):
                                    s_.wait_ge(sDVE.h, sDVE.n)
                                    qS.inc(s_.dma_start(out=dbg1, in_=ptm[:]), 16)
                                    qS.inc(s_.dma_start(out=dbg2, in_=dbw[:]), 16)
                                    qS.inc(s_.dma_start(out=dbg3, in_=dbx[:]), 16)
                                    qS.inc(s_.dma_start(out=dbg4, in_=dbx2[:]), 16)
                                    qS.inc(s_.dma_start(out=dbg5, in_=dbgstash), 16)

                        for grp in range(4 if FFNPART in ("all", "expert") else 0):
                            exps = [grp * 2 + i for i in range(2)]
                            _e4 = ExitStack()
                            with _e4:
                                bg0 = _e4.enter_context(nc.sbuf_tensor(U("bg0"), [128, 4096], BF16))[:]
                                bg1 = _e4.enter_context(nc.sbuf_tensor(U("bg1"), [128, 4096], BF16))[:]
                                bu0 = _e4.enter_context(nc.sbuf_tensor(U("bu0"), [128, 4096], BF16))[:]
                                bu1 = _e4.enter_context(nc.sbuf_tensor(U("bu1"), [128, 4096], BF16))[:]
                                ppw = _e4.enter_context(nc.psum_tensor(U("ppw"), [128, FCW], F32))[:]
                                ppg = _e4.enter_context(nc.psum_tensor(U("ppg"), [128, FCW], F32))[:]
                                ppu = _e4.enter_context(nc.psum_tensor(U("ppu"), [128, FCW], F32))[:]
                                gbufs = [(bg0, bu0), (bg1, bu1)]
                                gv = psum_guard_pt()
                                buf_guard = (sPE.n, sDVE.n, sACT.n)
                                gu_ready = {}
                                gu_consumed = {}

                                def load_gu(i):
                                    if i >= 2:
                                        return
                                    e = exps[i]
                                    guard = gu_consumed.get(i - 2)

                                    @block.gpsimd
                                    def _(g, e=e, i=i, guard=guard):
                                        if guard is not None:
                                            g.wait_ge(sPE.h, guard)
                                        else:
                                            g.wait_ge(sPE.h, buf_guard[0])
                                            g.wait_ge(sDVE.h, buf_guard[1])
                                            g.wait_ge(sACT.h, buf_guard[2])
                                        for (buf, nm) in ((gbufs[i % 2][0], "l0_wg"),
                                                          (gbufs[i % 2][1], "l0_wu")):
                                            src = dr[nm][e].rearrange(
                                                "(c p) o -> p c o", p=128)
                                            qG.inc(g.dma_start(
                                                out=buf.rearrange(
                                                    "p (c o) -> p c o", c=DT),
                                                in_=src), 16)
                                    gu_ready[i] = qG.n

                                load_gu(0)
                                load_gu(1)

                                for i in range(2):
                                    e = exps[i]
                                    bg, bu = gbufs[i % 2]
                                    act_b = sACT.n
                                    dve_b = sDVE.n
                                    sACT_expect = {gt: act_b + gt + 1 for gt in range(4)}
                                    sDVE_expect = {gt: dve_b + gt + 1 for gt in range(4)}

                                    @block.tensor
                                    def _(t, i=i, e=e, bg=bg, bu=bu,
                                          act_b=act_b, dve_b=dve_b,
                                          sACT_expect=sACT_expect,
                                          sDVE_expect=sDVE_expect):
                                        if i == 0:
                                            guard_new_psum(t, gv)
                                            t.wait_ge(qS.h, w0_done)
                                        t.wait_ge(qG.h, gu_ready[i])
                                        if i > 0:
                                            t.wait_ge(sDVE.h, dve_b)
                                            t.wait_ge(sACT.h, act_b)
                                        sPE.inc(t.matmul(
                                            ppw, onesb[0:1, :],
                                            w0em[0:1, e * FCW:(e + 1) * FCW],
                                            start=True, stop=True))
                                        for gt in range(4):
                                            if gt > 0:
                                                t.wait_ge(sDVE.h, sDVE_expect[gt - 1])
                                                t.wait_ge(sACT.h, sACT_expect[gt - 1])
                                            for c in range(DT):
                                                t.matmul(
                                                    ppg,
                                                    bg[:, c * 512 + gt * 128:
                                                       c * 512 + (gt + 1) * 128],
                                                    xhc[:, c * FCW:(c + 1) * FCW],
                                                    start=(c == 0), stop=(c == DT - 1))
                                            for c in range(DT):
                                                ins = t.matmul(
                                                    ppu,
                                                    bu[:, c * 512 + gt * 128:
                                                       c * 512 + (gt + 1) * 128],
                                                    xhc[:, c * FCW:(c + 1) * FCW],
                                                    start=(c == 0), stop=(c == DT - 1))
                                            sPE.inc(ins)

                                    @block.scalar
                                    def _(sc, pe_b=sPE.n - 4):
                                        for gt in range(4):
                                            sc.wait_ge(sPE.h, pe_b + gt + 1)
                                            sACT.inc(sc.activation(
                                                gsil[:], ppg, AF.Silu))

                                    @block.vector
                                    def _(v, i=i, act_b2=sACT.n - 4):
                                        for gt in range(4):
                                            v.wait_ge(sACT.h, act_b2 + gt + 1)
                                            v.tensor_tensor(gsil[:], gsil[:], ppu,
                                                            ALU.mult)
                                            sDVE.inc(v.tensor_tensor(
                                                hsc[:, (i * 4 + gt) * FCW:
                                                    (i * 4 + gt + 1) * FCW],
                                                gsil[:], ppw, ALU.mult))
                                    gu_consumed[i] = sPE.n
                                    load_gu(i + 2)

                            _e5 = ExitStack()
                            with _e5:
                                bd0 = _e5.enter_context(nc.sbuf_tensor(U("bd0"), [128, 4096], BF16))[:]
                                bd1 = _e5.enter_context(nc.sbuf_tensor(U("bd1"), [128, 4096], BF16))[:]
                                ppd0 = _e5.enter_context(nc.psum_tensor(U("ppd0"), [128, 512], F32))[:]
                                ppd1 = _e5.enter_context(nc.psum_tensor(U("ppd1"), [128, 512], F32))[:]
                                dbufs = [bd0, bd1]
                                gv = psum_guard_pt()
                                buf_guard = (sPE.n, sDVE.n, sACT.n)
                                dn_ready = {}
                                dn_consumed = {}

                                def load_dn(i):
                                    if i >= 2:
                                        return
                                    e = exps[i]
                                    guard = dn_consumed.get(i - 2)

                                    @block.gpsimd
                                    def _(g, e=e, i=i, guard=guard):
                                        if guard is not None:
                                            g.wait_ge(sPE.h, guard)
                                        else:
                                            g.wait_ge(sPE.h, buf_guard[0])
                                            g.wait_ge(sDVE.h, buf_guard[1])
                                            g.wait_ge(sACT.h, buf_guard[2])
                                        src = dr["l0_wd"][e].rearrange(
                                            "(c p) o -> p c o", p=128)
                                        qG.inc(g.dma_start(
                                            out=dbufs[i % 2].rearrange(
                                                "p (c o) -> p c o", c=4),
                                            in_=src), 16)
                                    dn_ready[i] = qG.n

                                load_dn(0)
                                load_dn(1)

                                for i in range(2):
                                    e = exps[i]
                                    bd = dbufs[i % 2]
                                    evac_b = sDVE.n
                                    dn_evac = {k: evac_b + k + 1 for k in range(8)}

                                    @block.tensor
                                    def _(t, i=i, bd=bd, evac_b=evac_b,
                                          dn_evac=dn_evac):
                                        if i == 0:
                                            guard_new_psum(t, gv)
                                        t.wait_ge(qG.h, dn_ready[i])
                                        if i > 0:
                                            t.wait_ge(sDVE.h, evac_b)
                                        for ot in range(DT):
                                            ps = [ppd0, ppd1][ot % 2]
                                            if ot >= 2:
                                                t.wait_ge(sDVE.h, dn_evac[ot - 2])
                                            for c in range(4):
                                                ins = t.matmul(
                                                    ps,
                                                    bd[:, c * 1024 + ot * 128:
                                                       c * 1024 + (ot + 1) * 128],
                                                    hsc[:, (i * 4 + c) * FCW:
                                                        (i * 4 + c + 1) * FCW],
                                                    start=(c == 0), stop=(c == 3))
                                            sPE.inc(ins)

                                    @block.vector
                                    def _(v, i=i, fc=fc, pe_b2=sPE.n - 8):
                                        for ot in range(DT):
                                            ps = [ppd0, ppd1][ot % 2]
                                            v.wait_ge(sPE.h, pe_b2 + ot + 1)
                                            sl = slice(fc * FCW, (fc + 1) * FCW)
                                            sDVE.inc(v.tensor_add(
                                                xr[ot][:, sl], xr[ot][:, sl], ps))
                                    dn_consumed[i] = sPE.n
                                    load_dn(i + 2)

                        if FFNPART not in ("all", "shared"):
                            continue
                        _e6 = ExitStack()
                        with _e6:
                            bsg = _e6.enter_context(nc.sbuf_tensor(U("bsg"), [128, 8192], BF16))[:]
                            bsu = _e6.enter_context(nc.sbuf_tensor(U("bsu"), [128, 8192], BF16))[:]
                            hss = hsc
                            psg = _e6.enter_context(nc.psum_tensor(U("psg"), [128, FCW], F32))[:]
                            psu = _e6.enter_context(nc.psum_tensor(U("psu"), [128, FCW], F32))[:]
                            gv = psum_guard_pt()
                            buf_guard = (sPE.n, sDVE.n, sACT.n)

                            @block.gpsimd
                            def _(g):
                                g.wait_ge(sPE.h, buf_guard[0])
                                g.wait_ge(sDVE.h, buf_guard[1])
                                g.wait_ge(sACT.h, buf_guard[2])
                                for (buf, nm) in ((bsg, "l0_wsg"), (bsu, "l0_wsu")):
                                    src = dr[nm].rearrange("(c p) o -> p c o", p=128)
                                    qG.inc(g.dma_start(
                                        out=buf.rearrange("p (c o) -> p c o", c=DT),
                                        in_=src), 16)
                            sg_ready = qG.n
                            sACT_e = {gt: sACT.n + gt + 1 for gt in range(8)}
                            sDVE_e = {gt: sDVE.n + gt + 1 for gt in range(8)}

                            @block.tensor
                            def _(t):
                                guard_new_psum(t, gv)
                                t.wait_ge(qG.h, sg_ready)
                                for gt in range(8):
                                    if gt > 0:
                                        t.wait_ge(sDVE.h, sDVE_e[gt - 1])
                                        t.wait_ge(sACT.h, sACT_e[gt - 1])
                                    for c in range(DT):
                                        t.matmul(
                                            psg,
                                            bsg[:, c * 1024 + gt * 128:
                                                c * 1024 + (gt + 1) * 128],
                                            xhc[:, c * FCW:(c + 1) * FCW],
                                            start=(c == 0), stop=(c == DT - 1))
                                    for c in range(DT):
                                        ins = t.matmul(
                                            psu,
                                            bsu[:, c * 1024 + gt * 128:
                                                c * 1024 + (gt + 1) * 128],
                                            xhc[:, c * FCW:(c + 1) * FCW],
                                            start=(c == 0), stop=(c == DT - 1))
                                    sPE.inc(ins)
                            pe_b3 = sPE.n - 8

                            @block.scalar
                            def _(sc):
                                for gt in range(8):
                                    sc.wait_ge(sPE.h, pe_b3 + gt + 1)
                                    sACT.inc(sc.activation(gsil[:], psg, AF.Silu))

                            @block.vector
                            def _(v, act_b3=sACT.n - 8):
                                for gt in range(8):
                                    v.wait_ge(sACT.h, act_b3 + gt + 1)
                                    sDVE.inc(v.tensor_tensor(
                                        hss[:, gt * FCW:(gt + 1) * FCW],
                                        gsil[:], psu, ALU.mult))
                            hss_done = sDVE.n

                        _e7 = ExitStack()
                        with _e7:
                            bsd = _e7.enter_context(nc.sbuf_tensor(U("bsd"), [128, 8192], BF16))[:]
                            pd0 = _e7.enter_context(nc.psum_tensor(U("pd0"), [128, 512], F32))[:]
                            pd1 = _e7.enter_context(nc.psum_tensor(U("pd1"), [128, 512], F32))[:]
                            gv2 = psum_guard_pt()
                            buf_guard2 = (sPE.n, sDVE.n, sACT.n)

                            @block.gpsimd
                            def _(g):
                                g.wait_ge(sPE.h, buf_guard2[0])
                                g.wait_ge(sDVE.h, buf_guard2[1])
                                g.wait_ge(sACT.h, buf_guard2[2])
                                src = dr["l0_wsd"].rearrange("(c p) o -> p c o", p=128)
                                qG.inc(g.dma_start(
                                    out=bsd.rearrange("p (c o) -> p c o", c=DT),
                                    in_=src), 16)
                            sd_ready = qG.n
                            sd_evac = {k: sDVE.n + k + 1 for k in range(8)}

                            @block.tensor
                            def _(t):
                                guard_new_psum(t, gv2)
                                t.wait_ge(qG.h, sd_ready)
                                t.wait_ge(sDVE.h, hss_done)
                                for ot in range(DT):
                                    ps = [pd0, pd1][ot % 2]
                                    if ot >= 2:
                                        t.wait_ge(sDVE.h, sd_evac[ot - 2])
                                    for c in range(DT):
                                        ins = t.matmul(
                                            ps,
                                            bsd[:, c * 1024 + ot * 128:
                                                c * 1024 + (ot + 1) * 128],
                                            hss[:, c * FCW:(c + 1) * FCW],
                                            start=(c == 0), stop=(c == DT - 1))
                                    sPE.inc(ins)

                            @block.vector
                            def _(v, fc=fc, pe_b4=sPE.n - 8):
                                for ot in range(DT):
                                    ps = [pd0, pd1][ot % 2]
                                    v.wait_ge(sPE.h, pe_b4 + ot + 1)
                                    sl = slice(fc * FCW, (fc + 1) * FCW)
                                    sDVE.inc(v.tensor_add(
                                        xr[ot][:, sl], xr[ot][:, sl], ps))

        stats_dram = nc.dram_tensor("stats_dram", [16, 512], F32).ap()

        def emit_spill_stats():
            for w in range(8):
                win = slice(w * 512, (w + 1) * 512)
                ln_stats(xr, "ln", 1e-5, win=win)
                std = sDVE.n

                @block.sync
                def _(s_, w=w, std=std):
                    s_.wait_ge(sDVE.h, std)
                    qS.inc(s_.dma_start(out=stats_dram[2 * w:2 * w + 1, :],
                                        in_=statm[0:1, 0:512]), 16)
                    qS.inc(s_.dma_start(out=stats_dram[2 * w + 1:2 * w + 2, :],
                                        in_=statr[0:1, 0:512]), 16)
            fin = sDVE.n

            @block.sync
            def _(s_):
                s_.wait_ge(sDVE.h, fin)
                for j in range(DT):
                    qS.inc(s_.dma_start(out=xspill[j * 128:(j + 1) * 128, :],
                                        in_=xr[j][:]), 16)

        def emit_mamba():
            TC = 512
            _m = ExitStack()
            with _m:
                A_sb = _m.enter_context(nc.sbuf_tensor(U("A_sb"), [128, 256], F32))[:]
                hstate = _m.enter_context(nc.sbuf_tensor(U("hstate"), [128, 256], F32))[:]
                xcm = _m.enter_context(nc.sbuf_tensor(U("xcm"), [128, 16 * TC], BF16))[:]
                zsm = _m.enter_context(nc.sbuf_tensor(U("zsm"), [128, 16 * TC], BF16))[:]
                Bsb = _m.enter_context(nc.sbuf_tensor(U("Bsb"), [128, 16 * TC], BF16))[:]
                Csb = _m.enter_context(nc.sbuf_tensor(U("Csb"), [128, 16 * TC], BF16))[:]
                ygm = _m.enter_context(nc.sbuf_tensor(U("ygm"), [128, 16 * TC], BF16))[:]
                xpm = _m.enter_context(nc.sbuf_tensor(U("xpm"), [128, 16 * (TC + 3)], BF16))[:]
                xdbl = _m.enter_context(nc.sbuf_tensor(U("xdbl"), [96, TC], F32))[:]
                xdbb = _m.enter_context(nc.sbuf_tensor(U("xdbb"), [96, TC], BF16))[:]
                rowst = _m.enter_context(nc.sbuf_tensor(U("rowst"), [1, 2 * TC], BF16))[:]
                dtt = _m.enter_context(nc.sbuf_tensor(U("dtt"), [128, TC], BF16))[:]
                utt = _m.enter_context(nc.sbuf_tensor(U("utt"), [128, TC], BF16))[:]
                dAb = _m.enter_context(nc.sbuf_tensor(U("dAb"), [128, TC], BF16))[:]
                dBb = _m.enter_context(nc.sbuf_tensor(U("dBb"), [128, TC], BF16))[:]
                hb = _m.enter_context(nc.sbuf_tensor(U("hb"), [128, TC], BF16))[:]
                hcb = _m.enter_context(nc.sbuf_tensor(U("hcb"), [128, TC], BF16))[:]
                sp1 = _m.enter_context(nc.sbuf_tensor(U("sp1"), [128, TC], F32))[:]
                sp2 = _m.enter_context(nc.sbuf_tensor(U("sp2"), [128, TC], F32))[:]
                ost = _m.enter_context(nc.sbuf_tensor(U("ost"), [128, 2 * TC], F32))[:]
                wxp = _m.enter_context(nc.sbuf_tensor(U("wxp"), [128, 16 * 96], BF16))[:]
                wdtp = _m.enter_context(nc.sbuf_tensor(U("wdtp"), [64, 2048], BF16))[:]
                xhbm = _m.enter_context(nc.sbuf_tensor(U("xhbm"), [128, DT * TC], BF16))[:]
                xrb0 = _m.enter_context(nc.sbuf_tensor(U("xrb0"), [128, TC], F32))[:]
                xrb1 = _m.enter_context(nc.sbuf_tensor(U("xrb1"), [128, TC], F32))[:]
                winp0 = _m.enter_context(nc.sbuf_tensor(U("winp0"), [128, 1024], BF16))[:]
                winp1 = _m.enter_context(nc.sbuf_tensor(U("winp1"), [128, 1024], BF16))[:]
                wob0 = _m.enter_context(nc.sbuf_tensor(U("wob0"), [128, 16 * 128], BF16))[:]
                wob1 = _m.enter_context(nc.sbuf_tensor(U("wob1"), [128, 16 * 128], BF16))[:]

                mg = (sPE.n, sDVE.n, sACT.n, qS.n)

                @block.gpsimd
                def _(g):
                    g.wait_ge(sPE.h, mg[0])
                    g.wait_ge(sDVE.h, mg[1])
                    g.wait_ge(sACT.h, mg[2])
                    g.wait_ge(qS.h, mg[3])
                    src = dr["l1_xpT"].rearrange("(c p) o -> p c o", p=128)
                    qG.inc(g.dma_start(
                        out=wxp.rearrange("p (c o) -> p c o", c=16), in_=src), 16)
                    qG.inc(g.dma_start(out=wdtp, in_=dr["l1_dtT"]), 16)
                mamba_w = qG.n

                @block.scalar
                def _(sc):
                    sc.wait_ge(qG.h, qG.n)
                    sc.wait_ge(qS.h, mg[3])
                    sc.activation(A_sb, smallw["l1_alog"][:], AF.Exp)
                    sACT.inc(sc.mul(A_sb, A_sb, -1.0))
                a_ready = sACT.n

                spill_done = qS.n

                for b in range(BPC):
                    for q in range(4):
                        wdx = b * 4 + q
                        win = slice(b * 2048 + q * 512, b * 2048 + (q + 1) * 512)

                        _p = ExitStack()
                        with _p:
                            psm = _p.enter_context(nc.psum_tensor(U("psmM"), [128, 512], F32))[:]
                            psr = _p.enter_context(nc.psum_tensor(U("psrM"), [128, 512], F32))[:]
                            gv = psum_guard_pt()
                            pe_at = sPE.n

                            @block.sync
                            def _(s_, wdx=wdx, pe_at=pe_at):
                                s_.wait_ge(qS.h, spill_done)
                                s_.wait_ge(sPE.h, pe_at)
                                qS.inc(s_.dma_start(out=statm[0:1, 0:512],
                                                    in_=stats_dram[2 * wdx:2 * wdx + 1, :]), 16)
                                qS.inc(s_.dma_start(out=statr[0:1, 0:512],
                                                    in_=stats_dram[2 * wdx + 1:2 * wdx + 2, :]), 16)
                            stats_in = qS.n

                            dve_b = sDVE.n
                            xr_done = {j: dve_b + (j + 1) for j in range(DT)}
                            xr_in = {}

                            @block.sync
                            def _(s_, win=win):
                                for j in range(DT):
                                    buf = [xrb0, xrb1][j % 2]
                                    if j >= 2:
                                        s_.wait_ge(sDVE.h, xr_done[j - 2])
                                    qS.inc(s_.dma_start(
                                        out=buf,
                                        in_=xspill[j * 128:(j + 1) * 128, win]), 16)
                                    xr_in[j] = qS.n

                            @block.tensor
                            def _(t):
                                guard_new_psum(t, gv)
                                t.wait_ge(qS.h, stats_in)
                                bcast_rows(t, psm, psr, slice(0, 512))
                            bc_done = sPE.n

                            @block.vector
                            def _(v, bc_done=bc_done):
                                v.wait_ge(sPE.h, bc_done)
                                for j in range(DT):
                                    buf = [xrb0, xrb1][j % 2]
                                    v.wait_ge(qS.h, xr_in[j])
                                    v.tensor_sub(sp1[:], buf, psm)
                                    sDVE.inc(v.tensor_tensor(
                                        xhbm[:, j * TC:(j + 1) * TC],
                                        sp1[:], psr, ALU.mult))
                        xh_done = sDVE.n

                        @block.vector
                        def _(v, q=q):
                            v.wait_ge(qS.h, mg[3])
                            for tile in range(16):
                                base = tile * (TC + 3)
                                if q == 0:
                                    v.memset(xpm[:, base:base + 3], 0.0)
                                else:
                                    v.tensor_copy(xpm[:, base:base + 3],
                                                  xpm[:, base + TC:base + TC + 3])
                            sDVE.inc(v.engine_nop())
                        halo_done = sDVE.n

                        _p = ExitStack()
                        with _p:
                            pq0 = _p.enter_context(nc.psum_tensor(U("mq0"), [128, 512], F32))[:]
                            pq1 = _p.enter_context(nc.psum_tensor(U("mq1"), [128, 512], F32))[:]
                            gv = psum_guard_pt()
                            bg = (sPE.n, sDVE.n, sACT.n)
                            w_in = {}
                            w_used = {}
                            act_b = sACT.n
                            evac_exp = {ot: act_b + ot + 1 for ot in range(32)}

                            def load_ip(ot):
                                if ot >= 32:
                                    return
                                guard = w_used.get(ot - 2)

                                @block.gpsimd
                                def _(g, ot=ot, guard=guard):
                                    if guard is not None:
                                        g.wait_ge(sPE.h, guard)
                                    else:
                                        g.wait_ge(sPE.h, bg[0])
                                        g.wait_ge(sDVE.h, bg[1])
                                        g.wait_ge(sACT.h, bg[2])
                                        g.wait_ge(qS.h, mg[3])
                                    src = dr["l1_inT"][:, ot * 128:(ot + 1) * 128] \
                                        .rearrange("(c p) o -> p c o", p=128)
                                    qG.inc(g.dma_start(
                                        out=[winp0, winp1][ot % 2]
                                        .rearrange("p (c o) -> p c o", c=DT),
                                        in_=src), 16)
                                w_in[ot] = qG.n

                            load_ip(0)
                            load_ip(1)
                            for ot in range(32):
                                ps = [pq0, pq1][ot % 2]

                                @block.tensor
                                def _(t, ot=ot, ps=ps):
                                    if ot == 0:
                                        guard_new_psum(t, gv)
                                        t.wait_ge(sDVE.h, xh_done)
                                    if ot >= 2:
                                        t.wait_ge(sACT.h, evac_exp[ot - 2])
                                    t.wait_ge(qG.h, w_in[ot])
                                    for c in range(DT):
                                        ins = t.matmul(
                                            ps, [winp0, winp1][ot % 2][:, c * 128:(c + 1) * 128],
                                            xhbm[:, c * TC:(c + 1) * TC],
                                            start=(c == 0), stop=(c == DT - 1))
                                    sPE.inc(ins)
                                w_used[ot] = sPE.n
                                load_ip(ot + 2)
                                mm_done = sPE.n

                                @block.scalar
                                def _(sc, ot=ot, ps=ps, mm_done=mm_done):
                                    sc.wait_ge(sPE.h, mm_done)
                                    if ot == 0:
                                        sc.wait_ge(sDVE.h, halo_done)
                                    if ot < 16:
                                        base = ot * (TC + 3)
                                        sACT.inc(sc.activation(
                                            xpm[:, base + 3:base + 3 + TC], ps,
                                            AF.Identity))
                                    else:
                                        sACT.inc(sc.activation(
                                            zsm[:, (ot - 16) * TC:(ot - 15) * TC], ps,
                                            AF.Silu))
                        inproj_done = sACT.n

                        cv_b = sDVE.n
                        cv_act = sACT.n
                        dve_per_tile = 4
                        pe_xc_guard = sPE.n

                        @block.vector
                        def _(v):
                            v.wait_ge(sACT.h, inproj_done)
                            for tile in range(16):
                                base = tile * (TC + 3)
                                if tile > 0:
                                    v.wait_ge(sACT.h, cv_act + tile)
                                v.tensor_scalar(sp1[:], xpm[:, base:base + TC],
                                                smallw["l1_conv"][:, tile * 4:tile * 4 + 1],
                                                None, ALU.mult)
                                for k in range(1, 4):
                                    v.scalar_tensor_tensor(
                                        sp1[:], xpm[:, base + k:base + k + TC],
                                        smallw["l1_conv"][:, tile * 4 + k:tile * 4 + k + 1],
                                        sp1[:], ALU.mult, ALU.add)
                                sDVE.inc(v.engine_nop())

                        @block.scalar
                        def _(sc):
                            sc.wait_ge(sPE.h, pe_xc_guard)
                            for tile in range(16):
                                sc.wait_ge(sDVE.h, cv_b + tile + 1)
                                sACT.inc(sc.activation(
                                    xcm[:, tile * TC:(tile + 1) * TC], sp1[:],
                                    AF.Silu,
                                    bias=smallw["l1_convb"][:, tile:tile + 1]))
                        conv_done = sACT.n

                        _p = ExitStack()
                        with _p:
                            pxd = _p.enter_context(nc.psum_tensor(U("pxd"), [128, 512], F32))[:]
                            gv = psum_guard_pt()

                            @block.tensor
                            def _(t):
                                guard_new_psum(t, gv)
                                t.wait_ge(sACT.h, conv_done)
                                t.wait_ge(qG.h, mamba_w)
                                for c in range(16):
                                    ins = t.matmul(
                                        pxd[0:96, :], wxp[:, c * 96:(c + 1) * 96],
                                        xcm[:, c * TC:(c + 1) * TC],
                                        start=(c == 0), stop=(c == 15))
                                sPE.inc(ins)
                            xp_done = sPE.n

                            @block.scalar
                            def _(sc):
                                sc.wait_ge(sPE.h, xp_done)
                                sACT.inc(sc.copy(xdbl, pxd[0:96, :]))
                            xd_act = sACT.n

                            @block.vector
                            def _(v):
                                v.wait_ge(sACT.h, xd_act)
                                sDVE.inc(v.tensor_copy(xdbb, xdbl))
                            xdb_done = sDVE.n

                            _p2 = ExitStack()
                            with _p2:
                                psB = _p2.enter_context(nc.psum_tensor(U("psB"), [128, 512], F32))[:]
                                psC = _p2.enter_context(nc.psum_tensor(U("psC"), [128, 512], F32))[:]
                                gv2 = psum_guard_pt()
                                mm_exp = {n: sPE.n + 2 * (n + 1) for n in range(16)}
                                ev_exp = {n: sACT.n + 2 * (n + 1) for n in range(16)}

                                @block.sync
                                def _(s_):
                                    for n in range(16):
                                        if n == 0:
                                            s_.wait_ge(sDVE.h, xdb_done)
                                        else:
                                            s_.wait_ge(sPE.h, mm_exp[n - 1])
                                        qS.inc(s_.dma_start(
                                            out=rowst[0:1, 0:TC],
                                            in_=xdbb[64 + n:65 + n, :]), 16)
                                        qS.inc(s_.dma_start(
                                            out=rowst[0:1, TC:2 * TC],
                                            in_=xdbb[80 + n:81 + n, :]), 16)

                                qs_b = qS.n - 32 * 16

                                @block.tensor
                                def _(t):
                                    for n in range(16):
                                        if n == 0:
                                            guard_new_psum(t, gv2)
                                        else:
                                            t.wait_ge(sACT.h, ev_exp[n - 1])
                                        t.wait_ge(qS.h, qs_b + 32 * (n + 1))
                                        t.matmul(psB, onesb[0:1, :],
                                                 rowst[0:1, 0:TC],
                                                 start=True, stop=True)
                                        ins = t.matmul(psC, onesb[0:1, :],
                                                       rowst[0:1, TC:2 * TC],
                                                       start=True, stop=True)
                                        sPE.inc(ins, 2)

                                @block.scalar
                                def _(sc):
                                    for n in range(16):
                                        sc.wait_ge(sPE.h, mm_exp[n])
                                        sc.activation(Bsb[:, n * TC:(n + 1) * TC],
                                                      psB, AF.Identity)
                                        sACT.inc(sc.activation(
                                            Csb[:, n * TC:(n + 1) * TC],
                                            psC, AF.Identity), 2)
                            bc_ready = sACT.n

                        _p = ExitStack()
                        with _p:
                            pd = _p.enter_context(nc.psum_tensor(U("pdM"), [128, 512], F32))[:]
                            py = _p.enter_context(nc.psum_tensor(U("pyM"), [128, 512], F32))[:]
                            gv = psum_guard_pt()
                            for tile in range(16):
                                dve_at = sDVE.n
                                act_at = sACT.n

                                @block.tensor
                                def _(t, tile=tile, dve_at=dve_at, act_at=act_at):
                                    if tile == 0:
                                        guard_new_psum(t, gv)
                                        t.wait_ge(sACT.h, bc_ready)
                                    else:
                                        t.wait_ge(sACT.h, act_at)
                                        t.wait_ge(sDVE.h, dve_at)
                                    sPE.inc(t.matmul(
                                        pd, wdtp[:, tile * 128:(tile + 1) * 128],
                                        xdbb[0:64, :], start=True, stop=True))
                                dt_mm = sPE.n

                                @block.scalar
                                def _(sc, tile=tile, dt_mm=dt_mm):
                                    sc.wait_ge(sPE.h, dt_mm)
                                    sc.activation(sp1[:], pd, AF.Abs,
                                                  bias=smallw["l1_dtb"][:, tile:tile + 1])
                                    sACT.inc(sc.activation(sp2[:], sp1[:], AF.Exp,
                                                           scale=-1.0))
                                e_done = sACT.n

                                @block.vector
                                def _(v, e_done=e_done):
                                    v.wait_ge(sACT.h, e_done)
                                    sDVE.inc(v.tensor_scalar_add(sp2[:], sp2[:], 1.0))
                                a_done = sDVE.n

                                @block.scalar
                                def _(sc, tile=tile, a_done=a_done):
                                    sc.wait_ge(sDVE.h, a_done)
                                    sc.activation(sp2[:], sp2[:], AF.Ln)
                                    sACT.inc(sc.activation(
                                        sp1[:], pd, AF.Relu,
                                        bias=smallw["l1_dtb"][:, tile:tile + 1]), 2)
                                sp_done = sACT.n

                                @block.vector
                                def _(v, tile=tile, sp_done=sp_done):
                                    v.wait_ge(sACT.h, sp_done)
                                    v.tensor_add(dtt[:], sp1[:], sp2[:])
                                    sDVE.inc(v.tensor_tensor(
                                        utt[:], dtt[:], xcm[:, tile * TC:(tile + 1) * TC],
                                        ALU.mult))
                                ut_done = sDVE.n

                                dve_c = sDVE.n
                                act_c = sACT.n
                                dve_n = {n: dve_c + n + 1 for n in range(16)}

                                @block.scalar
                                def _(sc, tile=tile, dve_c=dve_c, ut_done=ut_done):
                                    for n in range(16):
                                        if n == 0:
                                            sc.wait_ge(sDVE.h, ut_done)
                                        else:
                                            sc.wait_ge(sDVE.h, dve_c + n)
                                        sACT.inc(sc.activation(
                                            dAb[:], dtt[:], AF.Exp,
                                            scale=A_sb[:, tile * 16 + n:tile * 16 + n + 1]))

                                @block.vector
                                def _(v, tile=tile, q=q, act_c=act_c):
                                    for n in range(16):
                                        v.wait_ge(sACT.h, act_c + n + 1)
                                        v.tensor_tensor(
                                            dBb[:], utt[:],
                                            Bsb[:, n * TC:(n + 1) * TC], ALU.mult)
                                        col = tile * 16 + n
                                        if q == 0:
                                            v.tensor_tensor_scan(
                                                hb[:], dAb[:], dBb[:], 0.0,
                                                ALU.mult, ALU.add)
                                        else:
                                            v.tensor_tensor_scan(
                                                hb[:], dAb[:], dBb[:],
                                                hstate[:, col:col + 1],
                                                ALU.mult, ALU.add)
                                        if q < 3:
                                            v.drain()
                                            v.tensor_copy(hstate[:, col:col + 1],
                                                          hb[:, TC - 1:TC])
                                        sDVE.inc(v.tensor_tensor(
                                            hcb[:], hb[:],
                                            Csb[:, n * TC:(n + 1) * TC], ALU.mult))

                                @block.tensor
                                def _(t, dve_n=dve_n):
                                    for n in range(16):
                                        t.wait_ge(sDVE.h, dve_n[n])
                                        ins = t.matmul(py, ident[:], hcb[:],
                                                       start=(n == 0), stop=(n == 15))
                                    sPE.inc(ins)
                                y_mm = sPE.n

                                @block.vector
                                def _(v, tile=tile, y_mm=y_mm):
                                    v.wait_ge(sPE.h, y_mm)
                                    v.scalar_tensor_tensor(
                                        sp1[:], xcm[:, tile * TC:(tile + 1) * TC],
                                        smallw["l1_dvec"][:, tile:tile + 1],
                                        py, ALU.mult, ALU.add)
                                    sDVE.inc(v.tensor_tensor(
                                        ygm[:, tile * TC:(tile + 1) * TC],
                                        sp1[:], zsm[:, tile * TC:(tile + 1) * TC],
                                        ALU.mult))
                        yg_done = sDVE.n

                        _p = ExitStack()
                        with _p:
                            po0 = _p.enter_context(nc.psum_tensor(U("po0"), [128, 512], F32))[:]
                            po1 = _p.enter_context(nc.psum_tensor(U("po1"), [128, 512], F32))[:]
                            gv = psum_guard_pt()
                            bg2 = (sPE.n, sDVE.n, sACT.n)
                            ow_in = {}
                            ow_used = {}
                            ev_dve = {ot: sDVE.n + ot + 1 for ot in range(8)}
                            st_qs = {}

                            def load_op(ot):
                                if ot >= 8:
                                    return
                                guard = ow_used.get(ot - 2)

                                @block.gpsimd
                                def _(g, ot=ot, guard=guard):
                                    if guard is not None:
                                        g.wait_ge(sPE.h, guard)
                                    else:
                                        g.wait_ge(sPE.h, bg2[0])
                                        g.wait_ge(sDVE.h, bg2[1])
                                        g.wait_ge(sACT.h, bg2[2])
                                        g.wait_ge(qS.h, mg[3])
                                    src = dr["l1_outT"][:, ot * 128:(ot + 1) * 128] \
                                        .rearrange("(c p) o -> p c o", p=128)
                                    qG.inc(g.dma_start(
                                        out=[wob0, wob1][ot % 2]
                                        .rearrange("p (c o) -> p c o", c=16),
                                        in_=src), 16)
                                ow_in[ot] = qG.n

                            load_op(0)
                            load_op(1)
                            for ot in range(8):
                                ps = [po0, po1][ot % 2]

                                @block.tensor
                                def _(t, ot=ot, ps=ps):
                                    if ot == 0:
                                        guard_new_psum(t, gv)
                                        t.wait_ge(sDVE.h, yg_done)
                                    if ot >= 2:
                                        t.wait_ge(sDVE.h, ev_dve[ot - 2])
                                    t.wait_ge(qG.h, ow_in[ot])
                                    for c in range(16):
                                        ins = t.matmul(
                                            ps,
                                            [wob0, wob1][ot % 2][:, c * 128:(c + 1) * 128],
                                            ygm[:, c * TC:(c + 1) * TC],
                                            start=(c == 0), stop=(c == 15))
                                    sPE.inc(ins)
                                w_done = sPE.n
                                ow_used[ot] = sPE.n
                                load_op(ot + 2)

                                @block.vector
                                def _(v, ot=ot, ps=ps, w_done=w_done):
                                    v.wait_ge(sPE.h, w_done)
                                    if ot >= 2:
                                        v.wait_ge(qS.h, st_qs.get(ot - 2, 0))
                                    sDVE.inc(v.tensor_copy(
                                        ost[:, (ot % 2) * TC:(ot % 2 + 1) * TC], ps))
                                o_ev = sDVE.n

                                @block.sync
                                def _(s_, ot=ot, win=win, o_ev=o_ev):
                                    s_.wait_ge(sDVE.h, o_ev)
                                    qS.inc(s_.dma_start(
                                        out=mout[ot * 128:(ot + 1) * 128, win],
                                        in_=ost[:, (ot % 2) * TC:(ot % 2 + 1) * TC]), 16)
                                st_qs[ot] = qS.n
        def emit_l1ffn():
            reg2 = ExitStack()
            with reg2:
                src = [reg2.enter_context(nc.sbuf_tensor(U("x3_%d" % j), [128, T], F32))[:]
                       for j in range(DT)]
                pmsb = reg2.enter_context(nc.sbuf_tensor(U("pmsb"), [8, 1], F32))[:]
                _c = ExitStack()
                with _c:
                    mbf = _c.enter_context(nc.sbuf_tensor(U("mbf"), [128, T], F32))[:]
                    cg = (sPE.n, sDVE.n, sACT.n, qS.n)
                    for j in range(DT):
                        dve_at = sDVE.n

                        @block.sync
                        def _(s_, j=j, dve_at=dve_at):
                            if j == 0:
                                s_.wait_ge(qS.h, cg[3])
                                s_.wait_ge(sPE.h, cg[0])
                                s_.wait_ge(sDVE.h, cg[1])
                                s_.wait_ge(sACT.h, cg[2])
                                qS.inc(s_.dma_start(out=pmsb, in_=pmask), 16)
                            else:
                                s_.wait_ge(sDVE.h, dve_at)
                            qS.inc(s_.dma_start(out=src[j],
                                                in_=xspill[j * 128:(j + 1) * 128, :]), 16)
                            qS.inc(s_.dma_start(out=mbf,
                                                in_=mout[j * 128:(j + 1) * 128, :]), 16)
                        ld_q = qS.n

                        @block.vector
                        def _(v, j=j, ld_q=ld_q):
                            v.wait_ge(qS.h, ld_q)
                            sDVE.inc(v.tensor_add(src[j], src[j], mbf))
                        add_d = sDVE.n

                        @block.sync
                        def _(s_, j=j, add_d=add_d):
                            s_.wait_ge(sDVE.h, add_d)
                            qS.inc(s_.dma_start(out=x3comb[j * 128:(j + 1) * 128, :],
                                                in_=src[j]), 16)

                def apply_norm(kind, eps, gamma=None):
                    with nc.sbuf_tensor(U("lnt"), [128, 512], F32) as lnt:
                        for w in range(8):
                            win = slice(w * 512, (w + 1) * 512)
                            ln_stats(src, kind, eps, win=win)
                            _p = ExitStack()
                            with _p:
                                psm = _p.enter_context(nc.psum_tensor(U("nm"), [128, 512], F32))[:]
                                psr = _p.enter_context(nc.psum_tensor(U("nr"), [128, 512], F32))[:]
                                gv = psum_guard_pt()
                                dve_at = sDVE.n

                                @block.tensor
                                def _(t, dve_at=dve_at, kind=kind):
                                    guard_new_psum(t, gv)
                                    t.wait_ge(sDVE.h, dve_at)
                                    bcast_rows(t, psm, psr, slice(0, 512),
                                               which=("mr" if kind == "ln" else "r"))
                                bcd = sPE.n

                                @block.vector
                                def _(v, win=win, bcd=bcd, kind=kind, gamma=gamma):
                                    v.wait_ge(sPE.h, bcd)
                                    for j in range(DT):
                                        if kind == "ln":
                                            v.tensor_sub(lnt[:], src[j][:, win], psm)
                                            sDVE.inc(v.tensor_tensor(
                                                src[j][:, win], lnt[:], psr, ALU.mult))
                                        else:
                                            v.tensor_tensor(lnt[:], src[j][:, win],
                                                            psr, ALU.mult)
                                            sDVE.inc(v.tensor_scalar(
                                                src[j][:, win], lnt[:],
                                                smallw[gamma][:, j:j + 1], None,
                                                ALU.mult))

                apply_norm("ln", 1e-5)
                if DBG:
                    _dve_at0 = sDVE.n

                    @block.sync
                    def _(s_, _dve_at0=_dve_at0):
                        s_.wait_ge(sDVE.h, _dve_at0)
                        qS.inc(s_.dma_start(out=dbg11, in_=src[0]), 16)

                _gq = ExitStack()
                with _gq:
                    xhb = _gq.enter_context(nc.sbuf_tensor(U("xh4"), [128, DT * 2048], BF16))[:]
                    wq4 = _gq.enter_context(nc.sbuf_tensor(U("wq4"), [128, 1024], BF16))[:]
                    q2 = _gq.enter_context(nc.sbuf_tensor(U("q2g"), [128, 2048], BF16))[:]
                    k4 = _gq.enter_context(nc.sbuf_tensor(U("k4"), [128, 2048], BF16))[:]
                    v4 = _gq.enter_context(nc.sbuf_tensor(U("v4"), [128, 2048], BF16))[:]
                    vte1 = _gq.enter_context(nc.sbuf_tensor(U("vt4"), [128, 2080], BF16))[:]
                    attn0 = _gq.enter_context(nc.sbuf_tensor(U("a40"), [128, 2048], BF16))[:]
                    attn1 = _gq.enter_context(nc.sbuf_tensor(U("a41"), [128, 2048], BF16))[:]
                    woh0 = _gq.enter_context(nc.sbuf_tensor(U("woh0"), [64, 1024], BF16))[:]

                    bgq = (sPE.n, sDVE.n, sACT.n, qS.n)

                    qw_ready = {}
                    q_used = {}

                    def load_qw(bhp):
                        hp = bhp % 8
                        guard = q_used.get(bhp - 1)

                        @block.gpsimd
                        def _(g, hp=hp, guard=guard):
                            if guard is not None:
                                g.wait_ge(sPE.h, guard)
                            else:
                                g.wait_ge(sPE.h, bgq[0])
                                g.wait_ge(sDVE.h, bgq[1])
                                g.wait_ge(sACT.h, bgq[2])
                                g.wait_ge(qS.h, bgq[3])
                            src_ = dr["l1_qw"][:, hp * 128:(hp + 1) * 128] \
                                .rearrange("(c p) o -> p c o", p=128)
                            qG.inc(g.dma_start(
                                out=wq4.rearrange("p (c o) -> p c o", c=DT),
                                in_=src_), 16)
                        qw_ready[bhp] = qG.n

                    wo_ready = {}
                    wo_used = {}

                    def load_wo(bh):
                        h = GQA_PERM[bh % 16]
                        guard = wo_used.get(bh - 1)

                        @block.gpsimd
                        def _(g, h=h, guard=guard, bh=bh):
                            if guard is not None:
                                g.wait_ge(sPE.h, guard)
                            else:
                                g.wait_ge(sPE.h, bgq[0])
                                g.wait_ge(sDVE.h, bgq[1])
                                g.wait_ge(sACT.h, bgq[2])
                                g.wait_ge(qS.h, bgq[3])
                            qG.inc(g.dma_start(
                                out=woh0,
                                in_=dr["l1_ow"][h * 64:(h + 1) * 64, :]), 16)
                        wo_ready[bh] = qG.n

                    load_qw(0)
                    load_wo(0)

                    for b in range(BPC):
                        bs = slice(b * 2048, (b + 1) * 2048)
                        pe_cast = sPE.n

                        @block.vector
                        def _(v, bs=bs, pe_cast=pe_cast):
                            v.wait_ge(sPE.h, pe_cast)
                            for j in range(DT):
                                sDVE.inc(v.tensor_copy(
                                    xhb[:, j * 2048:(j + 1) * 2048], src[j][:, bs]))
                        xh_done = sDVE.n

                        kv_state = [None, None]

                        def emit_kv(kt):
                            _p = ExitStack()
                            with _p:
                                wkv4 = _p.enter_context(nc.sbuf_tensor(U("wkv4"), [128, 2048], BF16))[:]
                                kp0 = _p.enter_context(nc.psum_tensor(U("kp0"), [128, 512], F32))[:]
                                kp1 = _p.enter_context(nc.psum_tensor(U("kp1"), [128, 512], F32))[:]
                                gv = psum_guard_pt()
                                bgk = (sPE.n, sDVE.n, sACT.n, qS.n)
                                pe_k0 = sPE.n

                                @block.gpsimd
                                def _(g):
                                    g.wait_ge(sPE.h, bgk[0])
                                    g.wait_ge(sDVE.h, bgk[1])
                                    g.wait_ge(sACT.h, bgk[2])
                                    g.wait_ge(qS.h, bgk[3])
                                    for wi, nm in enumerate(("l1_kw", "l1_vw")):
                                        if wi == 1:
                                            g.wait_ge(sPE.h, pe_k0 + 4)
                                        src_ = dr[nm].rearrange("(c p) o -> p c o", p=128)
                                        qG.inc(g.dma_start(
                                            out=wkv4.rearrange("p (c o) -> p c o", c=DT),
                                            in_=src_), 16)
                                kv_w0 = qG.n - 32
                                act_k = sACT.n
                                evk = {k: act_k + k + 1 for k in range(8)}

                                @block.tensor
                                def _(t, kt=kt):
                                    guard_new_psum(t, gv)
                                    t.wait_ge(sDVE.h, xh_done)
                                    k_ = 0
                                    for wi in range(2):
                                        t.wait_ge(qG.h, kv_w0 + 16 * (wi + 1))
                                        for fc in range(4):
                                            ps = [kp0, kp1][k_ % 2]
                                            if k_ >= 2:
                                                t.wait_ge(sACT.h, evk[k_ - 2])
                                            for c in range(DT):
                                                ins = t.matmul(
                                                    ps,
                                                    wkv4[:, c * 256 + kt * 128:
                                                         c * 256 + (kt + 1) * 128],
                                                    xhb[:, c * 2048 + fc * 512:
                                                        c * 2048 + (fc + 1) * 512],
                                                    start=(c == 0), stop=(c == DT - 1))
                                            sPE.inc(ins)
                                            k_ += 1

                                @block.scalar
                                def _(sc, kt=kt, pe_k0=pe_k0):
                                    k_ = 0
                                    for dst, bnm in ((k4, "l1_kb"), (v4, "l1_vb")):
                                        for fc in range(4):
                                            ps = [kp0, kp1][k_ % 2]
                                            sc.wait_ge(sPE.h, pe_k0 + k_ + 1)
                                            sACT.inc(sc.activation(
                                                dst[:, fc * 512:(fc + 1) * 512],
                                                ps, AF.Identity,
                                                bias=smallw[bnm][:, kt:kt + 1]))
                                            k_ += 1
                            kv_state[0] = sACT.n
                            kv_state[1] = qS.n
                        vte_kt = [None]

                        def build_vte(kt):
                            _pv = ExitStack()
                            with _pv:
                                pst = _pv.enter_context(nc.psum_tensor(U("pst4"), [128, 1024], BF16))[:]
                                gvv = psum_guard_pt()
                                pe_at = sPE.n
                                dve_b0 = sDVE.n

                                @block.vector
                                def _(v, pe_at=pe_at, kvd=kv_state[0]):
                                    v.wait_ge(sACT.h, kvd)
                                    v.wait_ge(sPE.h, pe_at)
                                    sDVE.inc(v.memset(vte1[:], 1.0))

                                @block.tensor
                                def _(t, kt=kt, dve_b0=dve_b0):
                                    guard_new_psum(t, gvv)
                                    for scn in range(16):
                                        t.wait_ge(sDVE.h, dve_b0 + 1 + scn)
                                        sPE.inc(t.transpose(
                                            pst[:, 0:128],
                                            v4[:, scn * 128:(scn + 1) * 128],
                                            ident[:]))
                                tr_b = sPE.n - 16

                                @block.vector
                                def _(v, tr_b=tr_b):
                                    for scn in range(16):
                                        v.wait_ge(sPE.h, tr_b + scn + 1)
                                        base = scn * 130
                                        v.tensor_copy(vte1[:, base:base + 64],
                                                      pst[:, 0:64])
                                        sDVE.inc(v.tensor_copy(
                                            vte1[:, base + 65:base + 129],
                                            pst[:, 64:128]))
                            vte_kt[0] = kt

                        for hp in range(8):
                            bhp = b * 8 + hp
                            _p = ExitStack()
                            with _p:
                                qp0 = _p.enter_context(nc.psum_tensor(U("qp0"), [128, 512], F32))[:]
                                qp1 = _p.enter_context(nc.psum_tensor(U("qp1"), [128, 512], F32))[:]
                                gv = psum_guard_pt()
                                act_q = sACT.n
                                evq = {k: act_q + k + 1 for k in range(4)}

                                @block.tensor
                                def _(t, bhp=bhp):
                                    guard_new_psum(t, gv)
                                    t.wait_ge(sDVE.h, xh_done)
                                    t.wait_ge(qG.h, qw_ready[bhp])
                                    for fc in range(4):
                                        ps = [qp0, qp1][fc % 2]
                                        if fc >= 2:
                                            t.wait_ge(sACT.h, evq[fc - 2])
                                        for c in range(DT):
                                            ins = t.matmul(
                                                ps, wq4[:, c * 128:(c + 1) * 128],
                                                xhb[:, c * 2048 + fc * 512:
                                                    c * 2048 + (fc + 1) * 512],
                                                start=(c == 0), stop=(c == DT - 1))
                                        sPE.inc(ins)
                                q_used[bhp] = sPE.n
                                if bhp + 1 <= BPC * 8 - 1:
                                    load_qw(bhp + 1)
                                pe_q0 = sPE.n - 4

                                @block.scalar
                                def _(sc, hp=hp, pe_q0=pe_q0):
                                    for fc in range(4):
                                        ps = [qp0, qp1][fc % 2]
                                        sc.wait_ge(sPE.h, pe_q0 + fc + 1)
                                        sACT.inc(sc.activation(
                                            q2[:, fc * 512:(fc + 1) * 512], ps,
                                            AF.Identity,
                                            bias=smallw["l1_qb"][:, hp:hp + 1]))
                            q_done = sACT.n

                            for h2 in range(2):
                                h = hp * 2 + h2
                                bh = b * 16 + h
                                kvh = GQA_PERM[h] // 4
                                kt = kvh // 2
                                kr = slice((kvh % 2) * 64, (kvh % 2) * 64 + 64)
                                hr = slice(h2 * 64, h2 * 64 + 64)
                                if vte_kt[0] != kt:
                                    emit_kv(kt)
                                    build_vte(kt)
                                if DBG and b == 0 and hp == 0 and h2 == 0:

                                    @block.gpsimd
                                    def _(g, qd=q_done, kvd=kv_state[0]):
                                        g.wait_ge(sACT.h, max(qd, kvd))
                                        qG.inc(g.dma_start(out=dbg13, in_=q2[:]), 16)
                                        qG.inc(g.dma_start(out=dbg14, in_=k4[:]), 16)
                                        qG.inc(g.dma_start(out=dbg15, in_=v4[:]), 16)
                                _p2 = ExitStack()
                                with _p2:
                                    pq0 = _p2.enter_context(nc.psum_tensor(U("g0"), [128, 1024], F32))[:]
                                    pq1 = _p2.enter_context(nc.psum_tensor(U("g1"), [128, 1024], F32))[:]
                                    pctx = _p2.enter_context(nc.psum_tensor(U("gc"), [65, 2048], F32))[:]
                                    qk_ps = [pq0, pq1]
                                    attn_sb = [attn0, attn1]
                                    gva = psum_guard_pt()
                                    base_pe = sPE.n
                                    base_act = sACT.n
                                    dve_at = sDVE.n

                                    def pe_cnt(scn, half, base_pe=base_pe):
                                        c = 0
                                        for s_ in range(scn):
                                            c += 4 if s_ == 0 else 8
                                        c += (half + 1) * 2
                                        return base_pe + c

                                    ksrc = k4
                                    kr = slice(h2 * 64, h2 * 64 + 64)
                                    kv_qs = kv_state[1]

                                    @block.tensor
                                    def _(t, h2=h2, hr=hr, kr=kr, kt=kt, kvh=kvh,
                                          base_act=base_act, dve_at=dve_at,
                                          ksrc=ksrc, kv_qs=kv_qs):
                                        guard_new_psum(t, gva)
                                        t.wait_ge(sDVE.h, dve_at)
                                        t.wait_ge(sACT.h, q_done)
                                        t.wait_ge(qS.h, kv_qs)
                                        for scn in range(16):
                                            for half in range(2):
                                                if scn >= 2:
                                                    t.wait_ge(
                                                        sACT.h,
                                                        base_act + (scn - 2) * 2 + half + 1)
                                                for q4 in range(2):
                                                    ins = t.matmul(
                                                        qk_ps[half][:, q4 * 512:(q4 + 1) * 512],
                                                        ksrc[kr, scn * 128:(scn + 1) * 128],
                                                        q2[hr, half * 1024 + q4 * 512:
                                                           half * 1024 + (q4 + 1) * 512],
                                                        start=True, stop=True)
                                                sPE.inc(ins, 2)
                                            if scn >= 1:
                                                pscn = scn - 1
                                                t.wait_ge(sACT.h, base_act + pscn * 2 + 2)
                                                vsl = pscn * 130 + (kvh % 2) * 65
                                                for q4 in range(4):
                                                    ins = t.matmul(
                                                        pctx[:, q4 * 512:(q4 + 1) * 512],
                                                        vte1[:, vsl:vsl + 65],
                                                        attn_sb[pscn % 2][:, q4 * 512:(q4 + 1) * 512],
                                                        start=(pscn == 0), stop=False)
                                                sPE.inc(ins, 4)
                                        t.wait_ge(sACT.h, base_act + 15 * 2 + 2)
                                        vsl = 15 * 130 + (kvh % 2) * 65
                                        for q4 in range(4):
                                            ins = t.matmul(
                                                pctx[:, q4 * 512:(q4 + 1) * 512],
                                                vte1[:, vsl:vsl + 65],
                                                attn_sb[1][:, q4 * 512:(q4 + 1) * 512],
                                                start=False, stop=True)
                                        sPE.inc(ins, 4)

                                    @block.scalar
                                    def _(sc):
                                        for scn in range(16):
                                            for half in range(2):
                                                sc.wait_ge(sPE.h, pe_cnt(scn, half))
                                                sACT.inc(sc.activation(
                                                    attn_sb[scn % 2][:, half * 1024:
                                                                     (half + 1) * 1024],
                                                    qk_ps[half][:], AF.Exp, scale=0.125))

                                    av_done = sPE.n
                                    dve_b = sDVE.n

                                    @block.vector
                                    def _(v):
                                        v.wait_ge(sPE.h, av_done)
                                        v.reciprocal(statm[0:1, 0:1024],
                                                     pctx[64:65, 0:1024])
                                        sDVE.inc(v.reciprocal(statr[0:1, 0:1024],
                                                              pctx[64:65, 1024:2048]))

                                    @block.tensor
                                    def _(t, dve_b=dve_b):
                                        t.wait_ge(sDVE.h, dve_b + 1)
                                        for q4 in range(4):
                                            ps = pq0 if q4 < 2 else pq1
                                            row = statm if q4 < 2 else statr
                                            sPE.inc(t.matmul(
                                                ps[0:64, (q4 % 2) * 512:(q4 % 2) * 512 + 512],
                                                onesf[:, 0:64],
                                                row[0:1, (q4 % 2) * 512:(q4 % 2) * 512 + 512],
                                                start=True, stop=True))
                                    rb_done = sPE.n

                                    @block.scalar
                                    def _(sc):
                                        sc.wait_ge(sPE.h, rb_done)
                                        sACT.inc(sc.copy(attn0[0:64, :], pctx[0:64, :]))
                                    ctx_done = sACT.n

                                    @block.vector
                                    def _(v):
                                        v.wait_ge(sACT.h, ctx_done)
                                        for q4 in range(4):
                                            ps = pq0 if q4 < 2 else pq1
                                            sl = slice(q4 * 512, (q4 + 1) * 512)
                                            psl = slice((q4 % 2) * 512,
                                                        (q4 % 2) * 512 + 512)
                                            sDVE.inc(v.tensor_tensor(
                                                attn0[0:64, sl], attn0[0:64, sl],
                                                ps[0:64, psl], ALU.mult))
                                ctx_ok = sDVE.n
                                dbgq = [0]
                                if DBG and b == 0 and hp == 0 and h2 == 0:

                                    @block.gpsimd
                                    def _(g, ctx_ok=ctx_ok):
                                        g.wait_ge(sDVE.h, ctx_ok)
                                        qG.inc(g.dma_start(out=dbg16,
                                                           in_=attn0[0:64, :]), 16)
                                    dbgq[0] = qG.n

                                _p3 = ExitStack()
                                with _p3:
                                    op0 = _p3.enter_context(nc.psum_tensor(U("op0"), [128, 512], F32))[:]
                                    op1 = _p3.enter_context(nc.psum_tensor(U("op1"), [128, 512], F32))[:]
                                    gv = psum_guard_pt()
                                    ev_d = {k: sDVE.n + k + 1 for k in range(32)}
                                    wo = woh0

                                    @block.tensor
                                    def _(t, bh=bh, wo=wo, dbgq=dbgq[0]):
                                        guard_new_psum(t, gv)
                                        t.wait_ge(sDVE.h, ctx_ok)
                                        if dbgq:
                                            t.wait_ge(qG.h, dbgq)
                                        t.wait_ge(qG.h, wo_ready[bh])
                                        k_ = 0
                                        for j in range(DT):
                                            for fc in range(4):
                                                ps = [op0, op1][k_ % 2]
                                                if k_ >= 2:
                                                    t.wait_ge(sDVE.h, ev_d[k_ - 2])
                                                ins = t.matmul(
                                                    ps, wo[:, j * 128:(j + 1) * 128],
                                                    attn0[0:64, fc * 512:(fc + 1) * 512],
                                                    start=True, stop=True)
                                                sPE.inc(ins)
                                                k_ += 1
                                    wo_used[bh] = sPE.n
                                    if bh + 1 <= BPC * 16 - 1:
                                        load_wo(bh + 1)
                                    pe_o0 = sPE.n - 32

                                    @block.vector
                                    def _(v, bs=bs, h=h, pe_o0=pe_o0):
                                        k_ = 0
                                        for j in range(DT):
                                            for fc in range(4):
                                                ps = [op0, op1][k_ % 2]
                                                v.wait_ge(sPE.h, pe_o0 + k_ + 1)
                                                ssl = slice(bs.start + fc * 512,
                                                            bs.start + (fc + 1) * 512)
                                                if h == 0:
                                                    sDVE.inc(v.scalar_tensor_tensor(
                                                        src[j][:, ssl], ps,
                                                        smallw["l1_ob"][:, j:j + 1],
                                                        src[j][:, ssl],
                                                        ALU.add, ALU.add))
                                                else:
                                                    sDVE.inc(v.tensor_add(
                                                        src[j][:, ssl],
                                                        src[j][:, ssl], ps))
                                                k_ += 1

                if DBG:
                    _dve_at2 = sDVE.n

                    @block.sync
                    def _(s_, _dve_at2=_dve_at2):
                        s_.wait_ge(sDVE.h, _dve_at2)
                        qS.inc(s_.dma_start(out=dbg12, in_=src[0]), 16)

                apply_norm("rms", 1e-6, gamma="l1_n1")
                if DBG:
                    _dve_at = sDVE.n

                    @block.sync
                    def _(s_, _dve_at=_dve_at):
                        s_.wait_ge(sDVE.h, _dve_at)
                        qS.inc(s_.dma_start(out=dbg6, in_=src[0]), 16)

                _r = ExitStack()
                with _r:
                    gw_sb = _r.enter_context(nc.sbuf_tensor(U("gw"), [128, 64], BF16))[:]
                    xhc1 = _r.enter_context(nc.sbuf_tensor(U("xhc1"), [128, DT * 128], BF16))[:]
                    ptm = _r.enter_context(nc.sbuf_tensor(U("ptm4"), [128, 8], F32))[:]
                    mx8 = _r.enter_context(nc.sbuf_tensor(U("mx84"), [128, 8], F32))[:]
                    msk8 = _r.enter_context(nc.sbuf_tensor(U("mk84"), [128, 8], F32))[:]
                    em0 = _r.enter_context(nc.sbuf_tensor(U("em0"), [128, 8], F32))[:]
                    wmt = _r.enter_context(nc.sbuf_tensor(U("wmt"), [128, 8], F32))[:]
                    sm2 = _r.enter_context(nc.sbuf_tensor(U("sm2"), [128, 1], F32))[:]
                    emtT = _r.enter_context(nc.sbuf_tensor(U("emtT"), [8, T], BF16))[:]
                    em1T = _r.enter_context(nc.sbuf_tensor(U("em1T"), [8, T], BF16))[:]
                    wmT = _r.enter_context(nc.sbuf_tensor(U("wmT"), [8, T], BF16))[:]
                    ones8 = _r.enter_context(nc.sbuf_tensor(U("on8"), [8, 512], BF16))[:]
                    Ssc = _r.enter_context(nc.sbuf_tensor(U("Ssc"), [8, 512], F32))[:]
                    v0t = _r.enter_context(nc.sbuf_tensor(U("v0t"), [8, 512], BF16))[:]
                    v1t = _r.enter_context(nc.sbuf_tensor(U("v1t"), [8, 512], BF16))[:]
                    carr = _r.enter_context(nc.sbuf_tensor(U("carr"), [8, 1], F32))[:]
                    vwb = _r.enter_context(nc.sbuf_tensor(U("vwb"), [8, 512], BF16))[:]
                    cnt8 = _r.enter_context(nc.sbuf_tensor(U("cnt8"), [8, 1], F32))[:]
                    cntm = _r.enter_context(nc.sbuf_tensor(U("cntm"), [8, 16], F32))[:]
                    offs = _r.enter_context(nc.sbuf_tensor(U("offs"), [8, 1], F32))[:]
                    agst = _r.enter_context(nc.sbuf_tensor(U("agst"), [16, 128], F32))[:]

                    bgr = (sPE.n, sDVE.n, sACT.n, qS.n)

                    @block.gpsimd
                    def _(g):
                        g.wait_ge(sPE.h, bgr[0])
                        g.wait_ge(sDVE.h, bgr[1])
                        g.wait_ge(sACT.h, bgr[2])
                        g.wait_ge(qS.h, bgr[3])
                        src_ = dr["l1_gate"].rearrange("(c p) o -> p c o", p=128)
                        qG.inc(g.dma_start(
                            out=gw_sb.rearrange("p (c o) -> p c o", c=DT),
                            in_=src_), 16)
                        g.memset(ones8[:], 1.0)
                        sGPS.inc(g.memset(agst[:], 0.0))
                    gw_in = qG.n
                    gps_d = sGPS.n

                    _rp = ExitStack()
                    with _rp:
                        plg = _rp.enter_context(nc.psum_tensor(U("plg4"), [128, 512], F32))[:]
                        ptr = _rp.enter_context(nc.psum_tensor(U("ptr4"), [8, 512], F32))[:]
                        gv = psum_guard_pt()

                        for tch in range(32):
                            csl = slice(tch * 128, (tch + 1) * 128)
                            pe_at = sPE.n

                            @block.vector
                            def _(v, csl=csl, pe_at=pe_at, tch=tch):
                                v.wait_ge(sPE.h, pe_at)
                                for c in range(DT):
                                    sDVE.inc(v.tensor_copy(
                                        xhc1[:, c * 128:(c + 1) * 128],
                                        src[c][:, csl]))
                            cast_d = sDVE.n

                            @block.tensor
                            def _(t, tch=tch, cast_d=cast_d):
                                if tch == 0:
                                    guard_new_psum(t, gv)
                                    t.wait_ge(qG.h, gw_in)
                                t.wait_ge(sDVE.h, cast_d)
                                for c in range(DT):
                                    ins = t.matmul(
                                        plg[:, 0:8], xhc1[:, c * 128:(c + 1) * 128],
                                        gw_sb[:, c * 8:(c + 1) * 8],
                                        start=(c == 0), stop=(c == DT - 1))
                                sPE.inc(ins)
                            lg_d = sPE.n

                            @block.scalar
                            def _(sc, lg_d=lg_d):
                                sc.wait_ge(sPE.h, lg_d)
                                sACT.inc(sc.activation(ptm[:], plg[:, 0:8],
                                                       AF.Sigmoid))
                            sg_d = sACT.n

                            @block.vector
                            def _(v, sg_d=sg_d):
                                v.wait_ge(sACT.h, sg_d)
                                v.max(out=mx8[:], in_=ptm[:])
                                v.drain()
                                v.tensor_tensor(sm2[:], mx8[:, 0:1], mx8[:, 1:2],
                                                ALU.add)
                                v.drain()
                                v.tensor_scalar_add(sm2[:], sm2[:], 1e-6)
                                v.drain()
                                v.reciprocal(sm2[:], sm2[:])
                                v.drain()
                                v.tensor_scalar(msk8[:], ptm[:], mx8[:, 1:2], None,
                                                ALU.is_ge)
                                v.drain()
                                v.tensor_scalar(em0[:], ptm[:], mx8[:, 0:1], None,
                                                ALU.is_ge)
                                v.drain()
                                v.tensor_scalar(wmt[:], ptm[:], sm2[:, 0:1], None,
                                                ALU.mult)
                                v.drain()
                                sDVE.inc(v.tensor_tensor(wmt[:], wmt[:], msk8[:],
                                                         ALU.mult))
                            rt_d = sDVE.n

                            for mat, dstT in ((msk8, emtT), (em0, em1T), (wmt, wmT)):
                                dve_now = sDVE.n

                                @block.tensor
                                def _(t, mat=mat, dve_now=dve_now):
                                    t.wait_ge(sDVE.h, max(dve_now, rt_d))
                                    sPE.inc(t.transpose(ptr[:, 0:128], mat[:],
                                                        identF[:]))
                                trd = sPE.n

                                @block.vector
                                def _(v, dstT=dstT, csl=csl, trd=trd):
                                    v.wait_ge(sPE.h, trd)
                                    sDVE.inc(v.tensor_copy(dstT[:, csl],
                                                           ptr[:, 0:128]))
                        fix_d = sDVE.n

                        @block.vector
                        def _(v):
                            v.wait_ge(sDVE.h, fix_d)
                            v.tensor_sub(em1T[:], emtT[:], em1T[:])
                            v.drain()
                            v.tensor_reduce(cnt8[:], emtT[:], mybir.AxisListType.X,
                                            ALU.add)
                            v.drain()
                            sDVE.inc(v.tensor_copy(agst[0:8, 0:1], cnt8[:]))
                        cnt_d = sDVE.n
                        if DBG:

                            @block.gpsimd
                            def _(g):
                                g.wait_ge(sDVE.h, cnt_d)
                                qG.inc(g.dma_start(out=dbg7, in_=emtT[:]), 16)
                                qG.inc(g.dma_start(out=dbg8, in_=em1T[:]), 16)

                        @block.gpsimd
                        def _(g):
                            g.wait_ge(sDVE.h, cnt_d)
                            g.wait_ge(sGPS.h, gps_d)
                            qG.inc(g.dma_start(out=ag_in_t.ap(), in_=agst[:]), 16)
                            g.wait_ge(qG.h, qG.n)
                            g.collective_compute(
                                "AllGather", ALU.bypass,
                                replica_groups=[list(range(8))],
                                ins=[ag_in_t.ap().opt()],
                                outs=[ag_out_t.ap().opt()],
                            ).then_inc(sCC.h, 1)
                            sCC.n += 1
                            g.wait_ge(sCC.h, sCC.n)
                            with nc.allow_non_contiguous_dma(
                                    reason="128 scalar gather, once"):
                                qG.inc(g.dma_start(
                                    out=cntm[:],
                                    in_=ag_out_t.ap()[:, 0:1]
                                    .rearrange("(c e) o -> c (e o)", e=16)), 16)
                        ag_d = qG.n

                        @block.vector
                        def _(v):
                            v.wait_ge(qG.h, ag_d)
                            v.tensor_scalar(cntm[:, 0:8], cntm[:, 0:8],
                                            pmsb[0:8, 0:1], None, ALU.mult)
                            v.drain()
                            sDVE.inc(v.engine_nop())
                        mskc_d = sDVE.n

                        @block.tensor
                        def _(t):
                            t.wait_ge(sDVE.h, mskc_d)
                            sPE.inc(t.transpose(ptr[:, 0:8], cntm[:, 0:8],
                                                identF[0:8, 0:8]))
                        trc_d = sPE.n

                        @block.vector
                        def _(v):
                            v.wait_ge(sPE.h, trc_d)
                            v.tensor_reduce(offs[:], ptr[:, 0:8],
                                            mybir.AxisListType.X, ALU.add)
                            v.drain()
                            sDVE.inc(v.tensor_scalar_add(carr[:], offs[:],
                                                         float(-CAP)))
                            v.drain()
                        if DBG:
                            _od = sDVE.n

                            @block.sync
                            def _(s_, _od=_od):
                                s_.wait_ge(sDVE.h, _od)
                                qS.inc(s_.dma_start(out=dbg9, in_=cntm[:]), 16)
                                qS.inc(s_.dma_start(out=dbg10, in_=offs[:]), 16)
                        for w in range(8):
                            win = slice(w * 512, (w + 1) * 512)
                            qs_pv = qS.n

                            @block.vector
                            def _(v, win=win, qs_pv=qs_pv):
                                v.wait_ge(qS.h, qs_pv)
                                v.tensor_tensor_scan(Ssc[:], ones8[:],
                                                     emtT[:, win], 0.0,
                                                     ALU.mult, ALU.add)
                                v.drain()
                                v.tensor_scalar(Ssc[:], Ssc[:], carr[0:8, 0:1],
                                                None, ALU.add)
                                v.drain()
                                v.tensor_copy(carr[:], Ssc[:, 511:512])
                                v.drain()
                                v.tensor_sub(v0t[:], Ssc[:], emtT[:, win])
                                v.drain()
                                v.tensor_scalar(v0t[:], v0t[:], 0.0, None,
                                                ALU.is_lt)
                                v.drain()
                                v.tensor_sub(v1t[:], Ssc[:], em1T[:, win])
                                v.drain()
                                v.tensor_scalar(v1t[:], v1t[:], 0.0, None,
                                                ALU.is_lt)
                                v.drain()
                                v.tensor_sub(vwb[:], emtT[:, win],
                                             em1T[:, win])
                                v.drain()
                                v.tensor_tensor(vwb[:], vwb[:], v0t[:],
                                                ALU.mult)
                                v.drain()
                                v.tensor_tensor(em1T[:, win], em1T[:, win],
                                                v1t[:], ALU.mult)
                                v.drain()
                                v.tensor_add(vwb[:], vwb[:], em1T[:, win])
                                v.drain()
                                sDVE.inc(v.tensor_tensor(vwb[:], vwb[:],
                                                         wmT[:, win],
                                                         ALU.mult))
                                v.drain()
                            vw_dve = sDVE.n

                            @block.sync
                            def _(s_, win=win, vw_dve=vw_dve):
                                s_.wait_ge(sDVE.h, vw_dve)
                                qS.inc(s_.dma_start(out=vw_dram[:, win],
                                                    in_=vwb[:]), 16)
                    vw_d = qS.n

                _mx = ExitStack()
                with _mx:
                    w1h = _mx.enter_context(nc.sbuf_tensor(U("w1h"), [128, 8 * 1024], BF16))[:]
                    w2h = _mx.enter_context(nc.sbuf_tensor(U("w2h"), [128, 8 * 1024], BF16))[:]
                    hX = _mx.enter_context(nc.sbuf_tensor(U("hX"), [128, 8 * 512], BF16))[:]
                    xhr = _mx.enter_context(nc.sbuf_tensor(U("xhr"), [128, 8 * 512], BF16))[:]
                    vwrow = _mx.enter_context(nc.sbuf_tensor(U("vwrow"), [1, 512], BF16))[:]
                    hgs = _mx.enter_context(nc.sbuf_tensor(U("hgs"), [128, 512], BF16))[:]
                    bgm = (sPE.n, sDVE.n, sACT.n, qS.n)
                    w_in = {}
                    w_used = {}

                    def load_w(idx):
                        if idx >= 16:
                            return
                        e, half = idx // 2, idx % 2
                        guard = w_used.get(idx - 1)

                        @block.gpsimd
                        def _(g, e=e, half=half, guard=guard):
                            if guard is not None:
                                g.wait_ge(sPE.h, guard)
                            else:
                                g.wait_ge(sPE.h, bgm[0])
                                g.wait_ge(sDVE.h, bgm[1])
                                g.wait_ge(sACT.h, bgm[2])
                                g.wait_ge(qS.h, bgm[3])
                            s1 = dr["l1_w1"][e][:, half * 1024:(half + 1) * 1024] \
                                .rearrange("(c p) o -> p c o", p=128)
                            qG.inc(g.dma_start(
                                out=w1h.rearrange("p (c o) -> p c o", c=DT),
                                in_=s1), 16)
                            s2 = dr["l1_w2"][e][half * 1024:(half + 1) * 1024, :] \
                                .rearrange("(c p) o -> p c o", p=128)
                            qG.inc(g.dma_start(
                                out=w2h.rearrange("p (c o) -> p c o", c=DT),
                                in_=s2), 16)
                        w_in[idx] = qG.n

                    last_pvw = [None]
                    for idx in range(16):
                        e, half = idx // 2, idx % 2
                        load_w(idx)

                        _pp = ExitStack()
                        with _pp:
                            ph0 = _pp.enter_context(nc.psum_tensor(U("ph0"), [128, 512], F32))[:]
                            ph1 = _pp.enter_context(nc.psum_tensor(U("ph1"), [128, 512], F32))[:]
                            pvw = _pp.enter_context(nc.psum_tensor(U("pvw"), [128, 512], F32))[:]
                            gv = psum_guard_pt()
                            for fc in range(8):
                                fsl = slice(fc * 512, (fc + 1) * 512)
                                pe_pre = sPE.n

                                @block.vector
                                def _(v, fsl=fsl, fc=fc, pe_pre=pe_pre):
                                    v.wait_ge(sPE.h, pe_pre)
                                    for c in range(DT):
                                        sDVE.inc(v.tensor_copy(
                                            xhr[:, c * 512:(c + 1) * 512],
                                            src[c][:, fsl]))
                                cast_d = sDVE.n
                                act_h = sACT.n

                                @block.sync
                                def _(s_, e=e, fsl=fsl,
                                      first=(idx == 0 and fc == 0),
                                      lp=last_pvw[0]):
                                    if first:
                                        s_.wait_ge(sPE.h, bgm[0])
                                        s_.wait_ge(sACT.h, bgm[2])
                                        s_.wait_ge(qS.h, max(bgm[3], vw_d))
                                    else:
                                        s_.wait_ge(sPE.h, lp)
                                    qS.inc(s_.dma_start(
                                        out=vwrow[0:1, :],
                                        in_=vw_dram[e:e + 1, fsl]), 16)
                                vw_in = qS.n
                                last_pvw[0] = sPE.n + 1

                                @block.tensor
                                def _(t, idx=idx, fc=fc, fsl=fsl,
                                      cast_d=cast_d, act_h=act_h, e=e,
                                      vw_in=vw_in):
                                    if fc == 0:
                                        guard_new_psum(t, gv)
                                        t.wait_ge(qG.h, w_in[idx])
                                    t.wait_ge(qS.h, vw_in)
                                    t.wait_ge(sDVE.h, cast_d)
                                    sPE.inc(t.matmul(
                                        pvw, onesb[0:1, :],
                                        vwrow[0:1, :],
                                        start=True, stop=True))
                                    for ht in range(8):
                                        ps = [ph0, ph1][ht % 2]
                                        if ht >= 2:
                                            t.wait_ge(sACT.h, act_h + ht - 1)
                                        for c in range(DT):
                                            ins = t.matmul(
                                                ps,
                                                w1h[:, c * 1024 + ht * 128:
                                                    c * 1024 + (ht + 1) * 128],
                                                xhr[:, c * 512:(c + 1) * 512],
                                                start=(c == 0), stop=(c == DT - 1))
                                        sPE.inc(ins)
                                pe_h0 = sPE.n - 8

                                @block.scalar
                                def _(sc, pe_h0=pe_h0, cast_d=cast_d):
                                    for ht in range(8):
                                        ps = [ph0, ph1][ht % 2]
                                        sc.wait_ge(sPE.h, pe_h0 + ht + 1)
                                        if ht >= 1:
                                            sc.wait_ge(sDVE.h, cast_d + ht)
                                        sACT.inc(sc.activation(hgs[:], ps,
                                                               AF.Gelu))

                                @block.vector
                                def _(v, act_h=act_h):
                                    for ht in range(8):
                                        v.wait_ge(sACT.h, act_h + ht + 1)
                                        sDVE.inc(v.tensor_tensor(
                                            hX[:, ht * 512:(ht + 1) * 512],
                                            hgs[:], pvw, ALU.mult))
                                hx_d = sDVE.n

                                add_exp = {k: sDVE.n + k + 1 for k in range(8)}

                                @block.tensor
                                def _(t, hx_d=hx_d):
                                    t.wait_ge(sDVE.h, hx_d)
                                    for ot in range(8):
                                        ps = [ph0, ph1][ot % 2]
                                        if ot >= 2:
                                            t.wait_ge(sDVE.h, add_exp[ot - 2])
                                        elif ot == 0:
                                            pass
                                        for c in range(DT):
                                            ins = t.matmul(
                                                ps,
                                                w2h[:, c * 1024 + ot * 128:
                                                    c * 1024 + (ot + 1) * 128],
                                                hX[:, c * 512:(c + 1) * 512],
                                                start=(c == 0), stop=(c == DT - 1))
                                        sPE.inc(ins)
                                pe_w2 = sPE.n - 8

                                @block.vector
                                def _(v, fsl=fsl, pe_w2=pe_w2):
                                    for ot in range(8):
                                        ps = [ph0, ph1][ot % 2]
                                        v.wait_ge(sPE.h, pe_w2 + ot + 1)
                                        sDVE.inc(v.tensor_add(
                                            src[ot][:, fsl], src[ot][:, fsl],
                                            ps))
                        w_used[idx] = sPE.n

                apply_norm("rms", 1e-6, gamma="l1_n2")
                _f = ExitStack()
                with _f:
                    fb = _f.enter_context(nc.sbuf_tensor(U("fbf"), [128, T], F32))[:]
                    fg = (sPE.n, sDVE.n, sACT.n, qS.n)
                    for j in range(DT):
                        dve_at = sDVE.n

                        @block.sync
                        def _(s_, j=j, dve_at=dve_at):
                            if j == 0:
                                s_.wait_ge(qS.h, fg[3])
                                s_.wait_ge(sPE.h, fg[0])
                                s_.wait_ge(sDVE.h, fg[1])
                                s_.wait_ge(sACT.h, fg[2])
                            else:
                                s_.wait_ge(sDVE.h, dve_at)
                            qS.inc(s_.dma_start(out=fb,
                                                in_=x3comb[j * 128:(j + 1) * 128, :]), 16)
                        ld_q = qS.n

                        @block.vector
                        def _(v, j=j, ld_q=ld_q):
                            v.wait_ge(qS.h, ld_q)
                            sDVE.inc(v.tensor_add(fb, fb, src[j]))
                        add_d = sDVE.n

                        @block.sync
                        def _(s_, j=j, add_d=add_d):
                            s_.wait_ge(sDVE.h, add_d)
                            qS.inc(s_.dma_start(out=out_t[j * 128:(j + 1) * 128, :],
                                                in_=fb), 16)

                @block.sync
                def _(s_):
                    s_.wait_ge(qS.h, qS.n)

        mamba_qs = None

        emit_l0_attn()
        if PARTIAL >= 2:
            emit_l0_ffn()
        if PARTIAL >= 3:
            emit_spill_stats()

        if PARTIAL <= 2:
            final_dve = sDVE.n

            @block.sync
            def _(s):
                s.wait_ge(sDVE.h, final_dve)
                for j in range(DT):
                    qS.inc(s.dma_start(out=out_t[j * 128:(j + 1) * 128, :],
                                       in_=xr[j][:]), 16)
                s.wait_ge(qS.h, qS.n)
            reg1.close()
        else:
            reg1.close()
            emit_mamba()
            if PARTIAL >= 4:
                emit_l1ffn()
            if PARTIAL == 3:
                _f = ExitStack()
                with _f:
                    fa = _f.enter_context(nc.sbuf_tensor(U("fa"), [128, T], F32))[:]
                    fb = _f.enter_context(nc.sbuf_tensor(U("fb"), [128, T], F32))[:]
                    all_q = qS.n
                    fg = (sPE.n, sDVE.n, sACT.n)
                    for j in range(DT):
                        @block.sync
                        def _(s_, j=j):
                            if j == 0:
                                s_.wait_ge(qS.h, all_q)
                                s_.wait_ge(sPE.h, fg[0])
                                s_.wait_ge(sDVE.h, fg[1])
                                s_.wait_ge(sACT.h, fg[2])
                            else:
                                s_.wait_ge(sDVE.h, sDVE.n)
                            qS.inc(s_.dma_start(out=fa, in_=xspill[j * 128:(j + 1) * 128, :]), 16)
                            qS.inc(s_.dma_start(out=fb, in_=mout[j * 128:(j + 1) * 128, :]), 16)
                        ld_q = qS.n

                        @block.vector
                        def _(v, ld_q=ld_q):
                            v.wait_ge(qS.h, ld_q)
                            sDVE.inc(v.tensor_add(fa, fa, fb))
                        add_d = sDVE.n

                        @block.sync
                        def _(s_, j=j, add_d=add_d):
                            s_.wait_ge(sDVE.h, add_d)
                            qS.inc(s_.dma_start(out=out_t[j * 128:(j + 1) * 128, :],
                                                in_=fa), 16)

                    @block.sync
                    def _(s_):
                        s_.wait_ge(qS.h, qS.n)

    return nc


def kernel(**inputs):
    w = _prep_weights(inputs)
    x = np.asarray(inputs["L0_attn_hidden_states"])
    nc = build_program()
    in_maps = []
    for c in range(NCORES):
        xc = np.ascontiguousarray(
            x[c * BPC:(c + 1) * BPC].transpose(2, 0, 1).reshape(D, T))
        m = {"x_in": xc,
             "pmask": (np.arange(8) < c).astype(np.float32)[:, None]}
        m.update(w)
        in_maps.append(m)
    res = run_bass_kernel_spmd(nc, in_maps, core_ids=list(range(NCORES)))
    out = np.empty((B, S, D), dtype=np.float32)
    for c in range(NCORES):
        oc = res.results[c]["out"]
        out[c * BPC:(c + 1) * BPC] = oc.reshape(D, BPC, S).transpose(1, 2, 0)
    return out
